# version 05
# baseline (speedup 1.0000x reference)
import numpy as np
import ml_dtypes
from contextlib import ExitStack

import concourse.bass as bass
import concourse.mybir as mybir
from concourse.bass_utils import run_bass_kernel_spmd

F32 = mybir.dt.float32
BF16 = mybir.dt.bfloat16
FP8 = mybir.dt.float8e4
AF = mybir.ActivationFunctionType
ALU = mybir.AluOpType
AX = mybir.AxisListType

CUSH = 70.0
EPS = 1e-5
NEG = -1e30

FULL_CFG = dict(H=2048, NH=16, TQ=1024, S=4096, FFI=8192)


def derive(cfg):
    c = dict(cfg)
    c["HC"] = c["H"] // 128
    c["SCALE"] = float(1.0 / np.sqrt(128.0))
    c["QGW"] = min(512, c["TQ"])
    c["NQG"] = c["TQ"] // c["QGW"]
    c["QSUBG"] = c["QGW"] // 128
    c["MEGA"] = min(512, c["S"])
    c["NMEGA"] = c["S"] // c["MEGA"]
    c["KSUB"] = c["MEGA"] // 128
    c["NKSLOT"] = c["S"] // 128
    c["NQS"] = c["TQ"] // 128
    c["OWN0"] = c["NKSLOT"] - c["NQS"]
    c["VFW"] = min(512, c["H"])
    c["VFG"] = c["H"] // c["VFW"]
    c["FFB"] = c["FFI"] // 128
    c["NCH"] = max(2, c["FFI"] // 2048)
    c["FFHALF"] = c["FFI"] // c["NCH"]
    c["DQ"] = min(2048, c["FFHALF"])
    c["NDQ"] = c["FFHALF"] // c["DQ"]
    return c


class Rec:
    ENGS = ("sync", "gpsimd", "tensor", "vector", "scalar")

    def __init__(self, nc):
        self.nc = nc
        self.stages = [[]]
        self.CN = {}
        self.es = ExitStack()

    def sem(self, name):
        s = self.es.enter_context(self.nc.semaphore(name))
        self.CN[s] = 0
        return s

    def barrier(self):
        if self.stages[-1]:
            self.stages.append([])

    def op(self, eng, fn, waits=(), incs=()):
        rw = tuple((s, int(v)) for (s, v) in waits)
        ri = []
        out = None
        for (s, k) in incs:
            self.CN[s] += k
            ri.append((s, k))
            if out is None:
                out = self.CN[s]
        self.stages[-1].append((eng, fn, rw, tuple(ri)))
        return out

    def build(self):
        for stage in self.stages:
            if not stage:
                continue
            with self.nc.Block() as blk:
                for engname in self.ENGS:
                    ops = [o for o in stage if o[0] == engname]
                    if not ops:
                        continue
                    def body(eng, ops=ops):
                        for (_, fn, waits, incs) in ops:
                            for (s, v) in waits:
                                eng.wait_ge(s, v)
                            inst = fn(eng)
                            for (s, k) in incs:
                                inst.then_inc(s, k)
                    getattr(blk, engname)(body)


def build_nc(cfg):
    c = derive(cfg)
    H, NH, TQ, S, FFI = c["H"], c["NH"], c["TQ"], c["S"], c["FFI"]
    HC, SCALE = c["HC"], c["SCALE"]
    QGW, NQG, QSUBG = c["QGW"], c["NQG"], c["QSUBG"]
    MEGA, NMEGA, KSUB = c["MEGA"], c["NMEGA"], c["KSUB"]
    NKSLOT, OWN0 = c["NKSLOT"], c["OWN0"]
    VFW, VFG = c["VFW"], c["VFG"]
    FFB, FFHALF, DQ, NDQ = c["FFB"], c["FFHALF"], c["DQ"], c["NDQ"]
    NCH = c["NCH"]
    HINV = float(1.0 / H)
    NHB = FFHALF // 128
    SEGW = max(TQ, MEGA)

    PHASES = int(cfg.get("phases", 3))
    FFNLITE = int(cfg.get("ffnlite", 3))
    nc = bass.Bass("TRN2", target_bir_lowering=False)
    R = Rec(nc)

    def din(name, shape, dt=F32):
        return nc.dram_tensor(name, list(shape), dt, kind="ExternalInput").ap()

    xs = din("xs", (H, S))
    wq_cs = din("wq_cs", (NH, 128, H))
    wk_cs = din("wk_cs", (NH, 128, H))
    wv_rs = din("wv_rs", (VFG, 128, HC * VFW))
    wo_cs = din("wo_cs", (HC, 128, H))
    wg_cs = din("wg_cs", (FFB, 128, H))
    wu_cs = din("wu_cs", (FFB, 128, H))
    wd_cs = din("wd_cs", (HC, 128, FFI))
    wg_rows = din("wg_rows", (FFB, 128, H))
    wu_rows = din("wu_rows", (FFB, 128, H))
    wd_rows = din("wd_rows", (HC, 128, FFI))
    bqkv_t = din("bqkv_t", (128, 3 * HC))
    bout_t = din("bout_t", (128, HC))
    bv_bc = din("bv_bc", (128, H), BF16)
    tri_c = din("tri_in", (128, 128))
    flag_c = din("flag_in", (128, NKSLOT))
    ind_b = din("ind_b", (128, NH * NH), BF16)
    e32_c = din("e32_in", (64, 8 * 128))
    id_in = din("id_in", (128, 128))

    dacc_hbm = nc.dram_tensor("dacc_hbm", [H, TQ], F32, kind="Internal").ap()
    resid_hbm = nc.dram_tensor("resid_hbm", [H, TQ], F32, kind="Internal").ap()
    outT = nc.dram_tensor("outT", [H, TQ], F32, kind="ExternalOutput").ap()
    DBG = bool(cfg.get("dbg"))
    if DBG:
        dbgK = nc.dram_tensor("dbgK", [NMEGA * 128, NH * MEGA], BF16, kind="ExternalOutput").ap()
        dbgL = nc.dram_tensor("dbgL", [NMEGA * 16, NQG * QGW], F32, kind="ExternalOutput").ap()
        dbgPT = nc.dram_tensor("dbgPT", [NMEGA * 128, NH * KSUB * QGW], BF16, kind="ExternalOutput").ap()

    es = R.es
    sbuf = lambda n, shp, dt: es.enter_context(nc.sbuf_tensor(n, list(shp), dt))
    mkps = lambda n: es.enter_context(nc.psum_tensor(n, [128, 512], F32))

    s_dm = R.sem("s_dm")
    s_dmg = R.sem("s_dmg")
    s_res0 = R.sem("s_res0")
    s_res1 = R.sem("s_res1")
    s_res = [s_res0, s_res1]
    s_out0 = R.sem("s_out0")
    s_out1 = R.sem("s_out1")
    s_out = [s_out0, s_out1]
    s_mm = R.sem("s_mm")
    s_v = R.sem("s_v")
    s_sc = R.sem("s_sc")
    s_rl = R.sem("s_rl")

    CN = R.CN

    class Dstream:
        def __init__(self, name, queue, npar=2):
            self.sems = [R.sem(f"{name}{i}") for i in range(npar)]
            self.queue = queue
            self.n = 0
        def load(self, fn, waits=()):
            s = self.sems[self.n % len(self.sems)]
            self.n += 1
            cnt = R.op(self.queue, fn, waits=waits, incs=((s, 16),))
            return (s, cnt)

    stx = Dstream("stx", "sync")
    strow = Dstream("strow", "sync")
    stm = Dstream("stm", "sync")
    swg = Dstream("swg", "gpsimd")
    swc = Dstream("swc", "gpsimd")
    swv = Dstream("swv", "gpsimd")

    def V(fn, waits=()):
        return R.op("vector", fn, waits=tuple(waits) + ((s_v, CN[s_v]),), incs=((s_v, 1),))

    def SC(fn, waits=()):
        return R.op("scalar", fn, waits=tuple(waits) + ((s_sc, CN[s_sc]),), incs=((s_sc, 1),))

    def T(fn, waits=(), mark=False):
        return R.op("tensor", fn, waits=waits, incs=((s_mm, 1),) if mark else ())

    Q_sb = sbuf("Q", [128, NH * TQ], BF16)
    bq_sb = sbuf("bq", [128, 3 * HC], F32)
    bo_sb = sbuf("bo", [128, HC], F32)
    tri_sb = sbuf("tri", [128, 128], F32)
    flag_sb = sbuf("flag", [128, NKSLOT], F32)
    indb_sb = sbuf("indb", [128, NH * NH], BF16)
    e32_sb = sbuf("e32", [64, 8 * 128], F32)
    ones1_sb = sbuf("ones1", [1, 128], F32)
    onesc_sb = sbuf("onesc", [128, 1], F32)
    stat_sb = sbuf("stat", [32, 64], F32)
    rowt_sb = sbuf("rowt", [32, 32], F32)
    biasrow_sb = sbuf("biasrow", [1, 4 * 32], F32)
    bias_sb = sbuf("biasx", [128, KSUB * NH], F32)
    facbc_sb = sbuf("facbc", [128, NH], F32)
    rl0_sb = sbuf("rl0", [1, 2 * 512], F32)
    rL_sb = sbuf("rL", [16, NQG * QGW], F32)
    Lacc_sb = sbuf("Lacc", [16, NQG * QGW], F32)
    id128_sb = sbuf("id128", [128, 128], F32)
    tqT_sb = sbuf("tqT", [128, 512], F32)
    tq8_sb = sbuf("tq8", [128, 512], FP8)
    tqb_sb = sbuf("tqb", [128, 512], BF16)
    bmt_sb = sbuf("bmt", [128, 16], F32)
    rsbt_sb = sbuf("rsbt", [128, 16], F32)
    s2bt_sb = sbuf("s2bt", [128, 16], F32)
    q8w_sb = sbuf("q8w", [128, 512], FP8)
    bmw_sb = sbuf("bmw", [128, 128], F32)
    rsw_sb = sbuf("rsw", [128, 128], F32)
    s2w_sb = sbuf("s2w", [128, 128], F32)
    rstA_sb = sbuf("rstA", [64, 128], F32)
    s2tA_sb = sbuf("s2tA", [64, 128], F32)

    C_QMAX, C_KMAX, C_SB, C_SBN, C_FAC, C_T0, C_T1, C_T2 = range(8)
    stc = lambda col: stat_sb[0:NH, col:col + 1]

    PS_S0 = mkps("ps_s0"); PS_S1 = mkps("ps_s1")
    PS_O0 = mkps("ps_o0"); PS_O1 = mkps("ps_o1")
    PS_L0 = mkps("ps_l0"); PS_L1 = mkps("ps_l1")
    PS_X0 = mkps("ps_x0"); PS_X1 = mkps("ps_x1")
    SBANK = [PS_S0, PS_S1]
    LBANK = [PS_L0, PS_L1]

    for apv, dst in ((bqkv_t, bq_sb), (bout_t, bo_sb), (tri_c, tri_sb),
                     (flag_c, flag_sb), (e32_c, e32_sb), (id_in, id128_sb)):
        R.op("sync", lambda e, apv=apv, dst=dst: e.dma_start(dst[:, :], apv),
             incs=((s_dm, 16),))
    R.op("gpsimd", lambda e: e.dma_start(indb_sb[:, :], ind_b), incs=((s_dmg, 16),))
    V(lambda e: e.memset(ones1_sb[:, :], 1.0))
    V(lambda e: e.memset(onesc_sb[:, :], 1.0))
    V(lambda e: e.memset(stat_sb[:, :], 0.0))
    V(lambda e: e.memset(rowt_sb[:, :], 0.0))
    V(lambda e: e.memset(rstA_sb[:, :], 0.0))
    V(lambda e: e.memset(s2tA_sb[:, :], 0.0))
    V(lambda e: e.memset(biasrow_sb[:, :], 0.0))
    cdm = CN[s_dm]
    cdw = CN[s_dmg]
    R.barrier()
    T(lambda e: e.nop(), waits=((s_dm, cdm), (s_dmg, cdw)))
    V(lambda e: e.nop(), waits=((s_dm, cdm), (s_dmg, cdw)))
    SC(lambda e: e.nop(), waits=((s_dm, cdm), (s_dmg, cdw)))

    def emit_quantT(src_f32_ap, dst_bf16_ap, width=512):
        nj = width // 32
        V(lambda e: e.transpose(tqT_sb[:, 0:width], src_f32_ap))
        V(lambda e: e.tensor_reduce(
            bmt_sb[:, 0:nj], tqT_sb[:, 0:width].rearrange("p (j b) -> p j b", j=nj),
            AX.X, ALU.max, apply_absolute_value=True))
        V(lambda e: e.tensor_scalar(s2bt_sb[:, 0:nj], bmt_sb[:, 0:nj],
                                    1.0 / 448.0, 1e-30, ALU.mult, ALU.max))
        V(lambda e: e.reciprocal(rsbt_sb[:, 0:nj], s2bt_sb[:, 0:nj]))
        V(lambda e: e.tensor_scalar_mul(rsbt_sb[:, 0:nj], rsbt_sb[:, 0:nj], 0.5))
        V(lambda e: e.tensor_scalar_mul(s2bt_sb[:, 0:nj], s2bt_sb[:, 0:nj], 2.0))
        V(lambda e: e.tensor_tensor(
            tq8_sb[:, 0:width], tqT_sb[:, 0:width],
            rsbt_sb[:, 0:nj].broadcast_to([128, nj, 32]), ALU.mult))
        V(lambda e: e.tensor_tensor(
            tqb_sb[:, 0:width], tq8_sb[:, 0:width],
            s2bt_sb[:, 0:nj].broadcast_to([128, nj, 32]), ALU.mult))
        V(lambda e: e.transpose(dst_bf16_ap, tqb_sb[:, 0:width]))

    def emit_wquant(K, colbuf, rowbuf, dqbuf, extra_waits=()):
        nkb = K // 32
        assert nkb <= 64
        V(lambda e: e.tensor_reduce(
            bmw_sb[:, 0:nkb], rowbuf[:, 0:K].rearrange("p (b k) -> p b k", k=32),
            AX.X, ALU.max, apply_absolute_value=True), waits=extra_waits)
        V(lambda e: e.tensor_scalar(s2w_sb[:, 0:nkb], bmw_sb[:, 0:nkb],
                                    1.0 / 448.0, 1e-30, ALU.mult, ALU.max))
        V(lambda e: e.reciprocal(rsw_sb[:, 0:nkb], s2w_sb[:, 0:nkb]))
        V(lambda e: e.tensor_scalar_mul(rsw_sb[:, 0:nkb], rsw_sb[:, 0:nkb], 0.5))
        vm = V(lambda e: e.tensor_scalar_mul(s2w_sb[:, 0:nkb], s2w_sb[:, 0:nkb], 2.0))
        T(lambda e: e.transpose(PS_L0[0:nkb, 0:128], rsw_sb[:, 0:nkb], id128_sb[:, :]),
          waits=((s_v, vm),))
        tm = T(lambda e: e.transpose(PS_L0[0:nkb, 128:256], s2w_sb[:, 0:nkb], id128_sb[:, :]),
               mark=True)
        V(lambda e: e.tensor_copy(rstA_sb[0:nkb, :], PS_L0[0:nkb, 0:128]),
          waits=((s_mm, tm),))
        vm = V(lambda e: e.tensor_copy(s2tA_sb[0:nkb, :], PS_L0[0:nkb, 128:256]))
        nch = K // 128
        for g4 in range(0, nch, 4):
            n4 = min(4, nch - g4)
            mmw = [(s_v, CN[s_v])]
            for i in range(n4):
                ch = g4 + i
                kb0 = 4 * ch
                g32 = kb0 // 32
                cb = (kb0 % 32) // 4
                T(lambda e, i=i, cb=cb, g32=g32: e.matmul(
                    PS_X0[:, i * 128:(i + 1) * 128],
                    e32_sb[g32 * 32:g32 * 32 + 32, cb * 128:(cb + 1) * 128],
                    rstA_sb[g32 * 32:g32 * 32 + 32, :], start=True, stop=True),
                  waits=mmw)
                mmw = ()
                tm = T(lambda e, i=i, cb=cb, g32=g32: e.matmul(
                    PS_X1[:, i * 128:(i + 1) * 128],
                    e32_sb[g32 * 32:g32 * 32 + 32, cb * 128:(cb + 1) * 128],
                    s2tA_sb[g32 * 32:g32 * 32 + 32, :], start=True, stop=True),
                  mark=(i == n4 - 1))
            seg = slice(g4 * 128, g4 * 128 + n4 * 128)
            V(lambda e, seg=seg, n4=n4: e.scalar_tensor_tensor(
                q8w_sb[:, 0:n4 * 128], colbuf[:, seg], 1.0, PS_X0[:, 0:n4 * 128],
                ALU.mult, ALU.mult), waits=((s_mm, tm),))
            V(lambda e, seg=seg, n4=n4: e.tensor_tensor(
                dqbuf[:, seg], q8w_sb[:, 0:n4 * 128], PS_X1[:, 0:n4 * 128], ALU.mult))
        return V(lambda e: e.nop())

    def emit_ln(tok0, ntok, xst, xsq, xn_write, xnf_consumer,
                sumbanks, bcbanks, srow_sb, SEG):
        ngrp = max(1, ntok // 512)
        gw = ntok // ngrp
        sq_marks = {}
        mm_marks = {}
        dtok = {}
        for cc in range(HC):
            w = [(s_mm, mm_marks[cc - 2])] if cc >= 2 else []
            dtok[cc] = stx.load(lambda e, cc=cc: e.dma_start(
                xst[cc % 2][:, 0:ntok], xs[cc * 128:(cc + 1) * 128, tok0:tok0 + ntok]),
               waits=w)
            sq_marks[cc] = SC(lambda e, cc=cc: e.activation(
                xsq[cc % 2][:, 0:ntok], xst[cc % 2][:, 0:ntok], AF.Square),
                waits=[dtok[cc]] + ([(s_mm, mm_marks[cc - 2])] if cc >= 2 else []))
            for gq in range(ngrp):
                T(lambda e, cc=cc, gq=gq: e.matmul(
                    sumbanks[gq][0][0:1, 0:gw], onesc_sb[:, :],
                    xst[cc % 2][:, gq * gw:(gq + 1) * gw],
                    start=(cc == 0), stop=(cc == HC - 1)),
                  waits=(dtok[cc],))
            for gq in range(ngrp):
                mk = T(lambda e, cc=cc, gq=gq: e.matmul(
                    sumbanks[gq][1][0:1, 0:gw], onesc_sb[:, :],
                    xsq[cc % 2][:, gq * gw:(gq + 1) * gw],
                    start=(cc == 0), stop=(cc == HC - 1)),
                  waits=((s_sc, sq_marks[cc]),), mark=(gq == ngrp - 1))
            mm_marks[cc] = CN[s_mm]
        R.barrier()
        smw = [(s_mm, CN[s_mm])]
        for gq in range(ngrp):
            sgq = slice(gq * gw, (gq + 1) * gw)
            V(lambda e, sgq=sgq, gq=gq: e.tensor_scalar_mul(
                srow_sb[0:1, sgq], sumbanks[gq][0][0:1, 0:gw], HINV), waits=smw)
            smw = ()
            V(lambda e, sgq=sgq, gq=gq: e.tensor_scalar_mul(
                srow_sb[0:1, 2 * SEG + gq * gw:2 * SEG + (gq + 1) * gw],
                sumbanks[gq][1][0:1, 0:gw], HINV))
        V(lambda e: e.tensor_mul(srow_sb[0:1, SEG:SEG + ntok],
                                 srow_sb[0:1, 0:ntok], srow_sb[0:1, 0:ntok]))
        V(lambda e: e.tensor_sub(srow_sb[0:1, 2 * SEG:2 * SEG + ntok],
                                 srow_sb[0:1, 2 * SEG:2 * SEG + ntok], srow_sb[0:1, SEG:SEG + ntok]))
        vm = V(lambda e: e.tensor_scalar_add(srow_sb[0:1, 2 * SEG:2 * SEG + ntok], srow_sb[0:1, 2 * SEG:2 * SEG + ntok], EPS))
        SC(lambda e: e.activation(srow_sb[0:1, 2 * SEG:2 * SEG + ntok], srow_sb[0:1, 2 * SEG:2 * SEG + ntok], AF.Ln),
           waits=((s_v, vm),))
        scm = SC(lambda e: e.activation(srow_sb[0:1, 2 * SEG:2 * SEG + ntok], srow_sb[0:1, 2 * SEG:2 * SEG + ntok],
                                        AF.Exp, scale=-0.5))
        vm = V(lambda e: e.scalar_tensor_tensor(
            srow_sb[0:1, 0:ntok], srow_sb[0:1, 0:ntok], -1.0, srow_sb[0:1, 2 * SEG:2 * SEG + ntok],
            ALU.mult, ALU.mult), waits=((s_sc, scm),))
        for gq in range(ngrp):
            br, bs = bcbanks[gq]
            T(lambda e, gq=gq, br=br: e.matmul(
                br[:, 0:gw], ones1_sb[:, :], srow_sb[0:1, 2 * SEG + gq * gw:2 * SEG + (gq + 1) * gw],
                start=True, stop=True), waits=((s_v, vm),))
            T(lambda e, gq=gq, bs=bs: e.matmul(
                bs[:, 0:gw], ones1_sb[:, :], srow_sb[0:1, gq * gw:(gq + 1) * gw],
                start=True, stop=True), mark=(gq == ngrp - 1))
        bcm = CN[s_mm]
        v_marks = {}
        for cc in range(HC):
            w = [(s_v, v_marks[cc - 2])] if cc >= 2 else []
            dtok[cc] = stx.load(lambda e, cc=cc: e.dma_start(
                xst[cc % 2][:, 0:ntok], xs[cc * 128:(cc + 1) * 128, tok0:tok0 + ntok]),
               waits=w)
            first = [dtok[cc], (s_mm, bcm)]
            for gq in range(ngrp):
                br, bs = bcbanks[gq]
                sgq = slice(gq * gw, (gq + 1) * gw)
                V(lambda e, cc=cc, sgq=sgq, br=br: e.tensor_tensor(
                    xst[cc % 2][:, sgq], xst[cc % 2][:, sgq], br[:, 0:gw], ALU.mult),
                  waits=first)
                first = ()
                V(lambda e, cc=cc, sgq=sgq, bs=bs: e.tensor_tensor(
                    xst[cc % 2][:, sgq], xst[cc % 2][:, sgq], bs[:, 0:gw], ALU.add))
            wdst = xn_write(cc)
            if wdst is not None:
                V(lambda e, cc=cc, wdst=wdst: e.tensor_copy(wdst, xst[cc % 2][:, 0:ntok]))
            if xnf_consumer is not None:
                xnf_consumer(cc, xst[cc % 2])
            v_marks[cc] = CN[s_v]
        return CN[s_v]


    def _early_exit(stacks):
        R.op("sync", lambda e: e.nop(),
             waits=((s_res0, CN[s_res0]), (s_res1, CN[s_res1]),
                    (s_dm, CN[s_dm]), (s_dmg, CN[s_dmg])))
        R.build()
        for st in stacks:
            st.close()
        es.close()
        nc.finalize()
        nc._final_counts = {getattr(s, 'name', str(s)): v for s, v in CN.items()}
        return nc, c

    esA = ExitStack()
    asb = lambda n, shp, dt: esA.enter_context(nc.sbuf_tensor(n, list(shp), dt))
    xdq_sb = asb("xdq", [128, HC * TQ], BF16)
    xstA = [asb(f"xstA{i}", [128, TQ], F32) for i in range(2)]
    srowA_sb = asb("srowA", [1, 3 * TQ], F32)

    esA1 = ExitStack()
    xn_sb = esA1.enter_context(nc.sbuf_tensor("xn", [128, HC * TQ], BF16))
    xsqA = [esA1.enter_context(nc.sbuf_tensor(f"xsqA{i}", [128, TQ], F32)) for i in range(2)]
    qsqb_sb = [esA1.enter_context(nc.sbuf_tensor(f"qsqb{i}", [128, TQ], BF16)) for i in range(2)]
    wqb_sb = [esA1.enter_context(nc.sbuf_tensor(f"wqb{i}", [128, H], BF16)) for i in range(2)]

    def a_consume(cc, buf):
        w5 = min(512, TQ)
        for gq in range(TQ // w5):
            emit_quantT(buf[:, gq * w5:(gq + 1) * w5],
                        xdq_sb[:, cc * TQ + gq * w5:cc * TQ + (gq + 1) * w5], width=w5)

    emit_ln(S - TQ, TQ, xstA, xsqA,
            lambda cc: xn_sb[:, cc * TQ:(cc + 1) * TQ],
            a_consume,
            sumbanks=[(PS_X0, PS_X1), (PS_S0, PS_S1)],
            bcbanks=[(PS_O0, PS_O1), (PS_X0, PS_X1)], srow_sb=srowA_sb, SEG=TQ)
    R.barrier()
    if PHASES == 11:
        return _early_exit([esA1, esA])

    evac_marks = {}
    for h in range(NH):
        w = [(s_v, evac_marks[(h - 2, NQG - 1)])] if h >= 2 else []
        wtok = swc.load(lambda e, h=h: e.dma_start(wqb_sb[h % 2][:, 0:H], wq_cs[h]), waits=w)
        for gq in range(NQG):
            bank = SBANK[(h * NQG + gq) % 2]
            waits = [wtok]
            if (h * NQG + gq) >= 2:
                waits.append((s_v, evac_marks[divmod(h * NQG + gq - 2, NQG)]))
            for k in range(HC):
                mk = T(lambda e, h=h, gq=gq, k=k, bank=bank: e.matmul(
                    bank[:, 0:QGW], wqb_sb[h % 2][:, k * 128:(k + 1) * 128],
                    xn_sb[:, k * TQ + gq * QGW:k * TQ + (gq + 1) * QGW],
                    start=(k == 0), stop=(k == HC - 1)),
                  waits=waits if k == 0 else (), mark=(k == HC - 1))
            evac_marks[(h, gq)] = V(lambda e, h=h, gq=gq, bank=bank: e.tensor_scalar_add(
                Q_sb[:, h * TQ + gq * QGW:h * TQ + (gq + 1) * QGW],
                bank[:, 0:QGW], bq_sb[:, h:h + 1]), waits=((s_mm, mk),))
    R.barrier()

    sqm = {}
    qtm = {}
    for h in range(NH):
        sqm[h] = V(lambda e, h=h: e.tensor_mul(
            qsqb_sb[h % 2][:, 0:TQ], Q_sb[:, h * TQ:(h + 1) * TQ], Q_sb[:, h * TQ:(h + 1) * TQ]),
            waits=[(s_mm, qtm[h - 2])] if h >= 2 else ())
        for gq in range(NQG):
            mk = T(lambda e, h=h, gq=gq: e.matmul(
                [PS_X0, PS_X1][gq][0:NH, 0:QGW], indb_sb[:, h * NH:h * NH + NH],
                qsqb_sb[h % 2][:, gq * QGW:(gq + 1) * QGW],
                start=(h == 0), stop=(h == NH - 1)),
              waits=((s_v, sqm[h]),), mark=True)
        qtm[h] = mk
    V(lambda e: e.reduce_max(stc(C_T0), PS_X0[0:NH, 0:QGW], AX.X), waits=((s_mm, mk),))
    if NQG > 1:
        V(lambda e: e.reduce_max(stc(C_T1), PS_X1[0:NH, 0:QGW], AX.X))
        V(lambda e: e.tensor_max(stc(C_T0), stc(C_T0), stc(C_T1)))
    vm = CN[s_v]
    SC(lambda e: e.activation(stc(C_T0), stc(C_T0), AF.Ln), waits=((s_v, vm),))
    SC(lambda e: e.activation(stc(C_QMAX), stc(C_T0), AF.Exp, scale=0.5))
    R.barrier()
    if PHASES == 12:
        return _early_exit([esA1, esA])
    esA1.close()

    esA2 = ExitStack()
    hh_sb = esA2.enter_context(nc.sbuf_tensor("hh", [128, NHB * TQ], BF16))
    wsl_sb = [esA2.enter_context(nc.sbuf_tensor(f"wsl{i}", [128, DQ], F32)) for i in range(2)]
    wrow_sb = [esA2.enter_context(nc.sbuf_tensor(f"wrow{i}", [128, DQ], F32)) for i in range(2)]
    dqw_sb = [esA2.enter_context(nc.sbuf_tensor(f"dqw{i}", [128, DQ], BF16)) for i in range(2)]
    sg_sb = esA2.enter_context(nc.sbuf_tensor("sg", [128, 2 * QGW], F32))
    hf_sb = esA2.enter_context(nc.sbuf_tensor("hf", [128, QGW], F32))
    dtl_sb = [esA2.enter_context(nc.sbuf_tensor(f"dtl{i}", [128, TQ], F32)) for i in range(2)]

    for half in range(NCH):
        use_marks = {}
        gate_mm = {}
        last_mm = {}
        for j in range(2 * NHB):
            ib = half * NHB + j // 2
            par = j % 2
            w = [(s_v, use_marks[j - 2])] if j >= 2 else []
            ctok = swg.load(lambda e, ib=ib, par=par, j=j: e.dma_start(
                wsl_sb[par][:, 0:H], (wg_cs if j % 2 == 0 else wu_cs)[ib]), waits=w)
            rtok = strow.load(lambda e, ib=ib, par=par, j=j: e.dma_start(
                wrow_sb[par][:, 0:H], (wg_rows if j % 2 == 0 else wu_rows)[ib]), waits=w)
            xw = [ctok, rtok]
            if j >= 2:
                xw.append((s_mm, last_mm[j - 2]))
            emit_wquant(H, wsl_sb[par], wrow_sb[par], dqw_sb[par],
                        extra_waits=tuple(xw))
            qm = CN[s_v]
            if FFNLITE < 2:
                last_mm[j] = CN[s_mm]
                use_marks[j] = CN[s_v]
                continue
            for gq in range(NQG):
                bank = [SBANK, [PS_O0, PS_O1]][par][gq]
                waits = [(s_v, qm), (s_sc, CN[s_sc])]
                for k in range(HC):
                    mk = T(lambda e, par=par, gq=gq, k=k, bank=bank: e.matmul(
                        bank[:, 0:QGW], dqw_sb[par][:, k * 128:(k + 1) * 128],
                        xdq_sb[:, k * TQ + gq * QGW:k * TQ + (gq + 1) * QGW],
                        start=(k == 0), stop=(k == HC - 1)),
                      waits=waits if k == 0 else (), mark=(k == HC - 1))
                if j % 2 == 0:
                    gate_mm[(ib, gq)] = mk
                    last_mm[j] = mk
                else:
                    last_mm[j] = mk
                    if FFNLITE < 3:
                        continue
                    gb = SBANK[gq]
                    ub = [PS_O0, PS_O1][gq]
                    scm = SC(lambda e, gb=gb, gq=gq: e.activation(
                        sg_sb[:, gq * QGW:(gq + 1) * QGW], gb[:, 0:QGW], AF.Silu),
                             waits=((s_mm, gate_mm[(ib, gq)]),))
                    V(lambda e, ub=ub, gq=gq: e.tensor_tensor(
                        hf_sb[:, 0:QGW], sg_sb[:, gq * QGW:(gq + 1) * QGW],
                        ub[:, 0:QGW], ALU.mult),
                      waits=((s_sc, scm), (s_mm, mk)))
                    wdt = min(512, QGW)
                    for sg2 in range(QGW // wdt):
                        loc = (ib % NHB) * TQ + gq * QGW + sg2 * wdt
                        emit_quantT(hf_sb[:, sg2 * wdt:(sg2 + 1) * wdt],
                                    hh_sb[:, loc:loc + wdt], width=wdt)
            use_marks[j] = CN[s_v]
        R.barrier()
        if PHASES == 13:
            return _early_exit([esA2, esA])
        duse = {}
        dmms = {}
        evmk = {}
        stmk = {}
        mk_gq = {}
        for m in range(HC):
            for qtr in range(NDQ):
                jj = m * NDQ + qtr
                par = jj % 2
                w = [(s_v, duse[jj - 2])] if jj >= 2 else []
                k0 = half * FFHALF + qtr * DQ
                ctok = swg.load(lambda e, m=m, k0=k0, par=par: e.dma_start(
                    wsl_sb[par][:, 0:DQ], wd_cs[m][:, k0:k0 + DQ]), waits=w)
                rtok = strow.load(lambda e, m=m, k0=k0, par=par: e.dma_start(
                    wrow_sb[par][:, 0:DQ], wd_rows[m][:, k0:k0 + DQ]), waits=w)
                xw = [ctok, rtok]
                if jj >= 2:
                    xw.append((s_mm, dmms[jj - 2]))
                emit_wquant(DQ, wsl_sb[par], wrow_sb[par], dqw_sb[par],
                            extra_waits=tuple(xw))
                qm = CN[s_v]
                for gq in range(NQG):
                    bank = SBANK[(m * NQG + gq) % 2]
                    waits = [(s_v, qm)]
                    if qtr == 0:
                        prev = m - (2 if NQG == 1 else 1)
                        if prev >= 0:
                            waits.append((s_v, evmk[prev]))
                    nk = DQ // 128
                    for k in range(nk):
                        kg = qtr * nk + k
                        mk = T(lambda e, par=par, gq=gq, k=k, kg=kg, bank=bank: e.matmul(
                            bank[:, 0:QGW], dqw_sb[par][:, k * 128:(k + 1) * 128],
                            hh_sb[:, kg * TQ + gq * QGW:kg * TQ + (gq + 1) * QGW],
                            start=(kg == 0), stop=(kg == NHB - 1)),
                          waits=waits if k == 0 else (), mark=(kg == NHB - 1))
                    mk_gq[gq] = mk
                duse[jj] = CN[s_v]
                dmms[jj] = CN[s_mm]
            if half > 0:
                dtok1 = stm.load(lambda e, m=m: e.dma_start(
                    dtl_sb[m % 2][:, 0:TQ], dacc_hbm[m * 128:(m + 1) * 128, :]),
                   waits=((s_res0, CN[s_res0]), (s_res1, CN[s_res1])) if m <= 1 else
                         [stmk[m - 2]])
            if half == NCH - 1:
                dtok2 = stm.load(lambda e, m=m: e.dma_start(
                    xstA[m % 2][:, 0:TQ], xs[m * 128:(m + 1) * 128, S - TQ:S]))
            for gq in range(NQG):
                bank = SBANK[(m * NQG + gq) % 2]
                sw = [stmk[m - 2]] if m >= 2 else []
                if half == 0:
                    V(lambda e, m=m, gq=gq, bank=bank: e.tensor_copy(
                        dtl_sb[m % 2][:, gq * QGW:(gq + 1) * QGW], bank[:, 0:QGW]),
                      waits=[(s_mm, mk_gq[gq])] + sw)
                else:
                    V(lambda e, m=m, gq=gq, bank=bank: e.tensor_tensor(
                        dtl_sb[m % 2][:, gq * QGW:(gq + 1) * QGW],
                        dtl_sb[m % 2][:, gq * QGW:(gq + 1) * QGW],
                        bank[:, 0:QGW], ALU.add),
                      waits=[(s_mm, mk_gq[gq]), dtok1] + sw)
                    if half == NCH - 1:
                        V(lambda e, m=m, gq=gq: e.tensor_tensor(
                            dtl_sb[m % 2][:, gq * QGW:(gq + 1) * QGW],
                            dtl_sb[m % 2][:, gq * QGW:(gq + 1) * QGW],
                            xstA[m % 2][:, gq * QGW:(gq + 1) * QGW], ALU.add),
                          waits=(dtok2,))
            evmk[m] = vm2 = CN[s_v]
            dst = dacc_hbm if half < NCH - 1 else resid_hbm
            stmk[m] = (s_res[m % 2], R.op("sync", lambda e, m=m, dst=dst: e.dma_start(
                dst[m * 128:(m + 1) * 128, :], dtl_sb[m % 2][:, 0:TQ]),
                waits=((s_v, vm2),), incs=((s_res[m % 2], 16),)))
        R.barrier()
        if PHASES == 14:
            return _early_exit([esA2, esA])
    esA2.close()
    esA.close()

    esB = ExitStack()
    bsb = lambda n, shp, dt: esB.enter_context(nc.sbuf_tensor(n, list(shp), dt))
    O_sb = bsb("O", [128, NH * TQ], F32)
    bvbc_sb = bsb("bvbc", [128, H], BF16)
    K_sb = bsb("K", [128, NH * MEGA], BF16)
    V_sb = bsb("V", [128, KSUB * H], BF16)
    xnm_sb = bsb("xnm", [128, HC * MEGA], BF16)
    wvb_sb = bsb("wvb", [128, (HC // 2) * VFW], BF16)
    wkb_sb = [bsb(f"wkb{i}", [128, H], BF16) for i in range(2)]
    PT_sb = [bsb(f"PT{i}", [128, QGW], BF16) for i in range(2)]
    ksq_sb = bsb("ksq", [128, MEGA], BF16)
    xstB = [bsb(f"xstB{i}", [128, MEGA], F32) for i in range(2)]
    xsqB = [bsb(f"xsqB{i}", [128, MEGA], F32) for i in range(2)]
    srowB_sb = bsb("srowB", [1, 3 * MEGA], F32)

    if PHASES < 2:
        R.op("sync", lambda e: e.nop(),
             waits=((s_res0, CN[s_res0]), (s_res1, CN[s_res1])))
        R.build()
        esB.close()
        es.close()
        nc.finalize()
        nc._final_counts = {getattr(s, 'name', str(s)): v for s, v in CN.items()}
        return nc, c
    resw = ((s_res0, CN[s_res0]), (s_res1, CN[s_res1]))
    R.op("sync", lambda e: e.nop(), waits=resw)
    R.op("gpsimd", lambda e: e.nop(), waits=resw)
    R.op("gpsimd", lambda e: e.dma_start(bvbc_sb[:, :], bv_bc), incs=((s_dmg, 16),))
    bvtok = (s_dmg, CN[s_dmg])
    T(lambda e: e.nop(), waits=resw)
    SC(lambda e: e.nop(), waits=resw)
    V(lambda e: e.memset(O_sb[:, :], 0.0), waits=resw)
    V(lambda e: e.memset(Lacc_sb[:, :], 0.0))
    R.barrier()

    for M in range(NMEGA):
        emit_ln(M * MEGA, MEGA, xstB, xsqB,
                lambda cc: xnm_sb[:, cc * MEGA:(cc + 1) * MEGA],
                None,
                sumbanks=[(PS_X0, PS_X1)],
                bcbanks=[(PS_X0, PS_X1)], srow_sb=srowB_sb, SEG=MEGA)
        R.barrier()
        kevac = {}
        for h in range(NH):
            w = [(s_v, kevac[h - 2])] if h >= 2 else []
            ktok = swc.load(lambda e, h=h: e.dma_start(wkb_sb[h % 2][:, 0:H], wk_cs[h]), waits=w)
            bank = SBANK[h % 2]
            waits = [ktok]
            if h >= 2:
                waits.append((s_v, kevac[h - 2]))
            for k in range(HC):
                mk = T(lambda e, h=h, k=k, bank=bank: e.matmul(
                    bank[:, 0:MEGA], wkb_sb[h % 2][:, k * 128:(k + 1) * 128],
                    xnm_sb[:, k * MEGA:(k + 1) * MEGA],
                    start=(k == 0), stop=(k == HC - 1)),
                  waits=waits if k == 0 else (), mark=(k == HC - 1))
            V(lambda e, h=h, bank=bank: e.tensor_scalar_add(
                K_sb[:, h * MEGA:(h + 1) * MEGA], bank[:, 0:MEGA], bq_sb[:, HC + h:HC + h + 1]),
              waits=((s_mm, mk),))
            kevac[h] = V(lambda e, h=h: e.tensor_mul(
                ksq_sb[:, 0:MEGA], K_sb[:, h * MEGA:(h + 1) * MEGA],
                K_sb[:, h * MEGA:(h + 1) * MEGA]))
            mk = T(lambda e, h=h: e.matmul(
                PS_O0[0:NH, 0:MEGA], indb_sb[:, h * NH:h * NH + NH], ksq_sb[:, 0:MEGA],
                start=(h == 0), stop=(h == NH - 1)),
              waits=((s_v, kevac[h]),), mark=(h == NH - 1))
        if DBG:
            R.op("sync", lambda e, M=M: e.dma_start(
                dbgK[M * 128:(M + 1) * 128, :], K_sb[:, :]),
                waits=((s_v, CN[s_v]),), incs=((s_dm, 16),))
        V(lambda e: e.reduce_max(stc(C_T0), PS_O0[0:NH, 0:MEGA], AX.X), waits=((s_mm, mk),))
        vm = CN[s_v]
        SC(lambda e: e.activation(stc(C_T0), stc(C_T0), AF.Ln), waits=((s_v, vm),))
        scm = SC(lambda e: e.activation(stc(C_KMAX), stc(C_T0), AF.Exp, scale=0.5))
        V(lambda e: e.tensor_mul(stc(C_T1), stc(C_QMAX), stc(C_KMAX)), waits=((s_sc, scm),))
        V(lambda e: e.tensor_scalar_mul(stc(C_T1), stc(C_T1), SCALE))
        if M == 0:
            V(lambda e: e.tensor_copy(stc(C_SB), stc(C_T1)))
        else:
            V(lambda e: e.tensor_max(stc(C_SBN), stc(C_SB), stc(C_T1)))
            vm = V(lambda e: e.tensor_sub(stc(C_T2), stc(C_SB), stc(C_SBN)))
            scm = SC(lambda e: e.activation(stc(C_FAC), stc(C_T2), AF.Exp),
                     waits=((s_v, vm),))
            V(lambda e: e.tensor_copy(stc(C_SB), stc(C_SBN)), waits=((s_sc, scm),))
        if M > 0:
            vm = V(lambda e: e.transpose(rowt_sb[:, :], stat_sb[0:32, C_FAC:C_FAC + 32]))
            tm = T(lambda e: e.matmul(PS_X1[:, 0:NH], ones1_sb[:, :],
                                      rowt_sb[0:1, 0:NH], start=True, stop=True),
                   waits=((s_v, vm),), mark=True)
            V(lambda e: e.tensor_copy(facbc_sb[:, :], PS_X1[:, 0:NH]), waits=((s_mm, tm),))
            for g in range(NQG):
                V(lambda e, g=g: e.tensor_scalar_mul(
                    Lacc_sb[0:NH, g * QGW:(g + 1) * QGW],
                    Lacc_sb[0:NH, g * QGW:(g + 1) * QGW], stc(C_FAC)),
                  waits=((s_dm, CN[s_dm]),))
            for h in range(NH):
                for g in range(NQG):
                    V(lambda e, h=h, g=g: e.tensor_scalar_mul(
                        O_sb[:, h * TQ + g * QGW:h * TQ + (g + 1) * QGW],
                        O_sb[:, h * TQ + g * QGW:h * TQ + (g + 1) * QGW],
                        facbc_sb[:, h:h + 1]))
        V(lambda e: e.transpose(rowt_sb[:, :], stat_sb[0:32, C_SB:C_SB + 32]))
        V(lambda e: e.tensor_scalar(biasrow_sb[0:1, 96:96 + NH], rowt_sb[0:1, 0:NH],
                                    -1.0, CUSH, ALU.mult, ALU.add))
        for ks in range(KSUB):
            V(lambda e, ks=ks, M=M: e.tensor_scalar_add(
                biasrow_sb[0:1, ks * NH:(ks + 1) * NH],
                biasrow_sb[0:1, 96:96 + NH],
                flag_sb[0:1, M * KSUB + ks:M * KSUB + ks + 1]))
        vm = CN[s_v]
        tm = T(lambda e: e.matmul(PS_X0[:, 0:NH * KSUB], ones1_sb[:, :],
                                  biasrow_sb[0:1, 0:NH * KSUB], start=True, stop=True),
               waits=((s_v, vm),), mark=True)
        V(lambda e: e.tensor_copy(bias_sb[:, 0:NH * KSUB], PS_X0[:, 0:NH * KSUB]),
          waits=((s_mm, tm),))
        bias_ready = CN[s_v]
        R.barrier()
        vevac = {}
        vln = {}
        nhalf = HC // 2 if HC > 1 else 1
        for F4 in range(VFG):
            for sub in range(2 if HC > 1 else 1):
                idx = F4 * 2 + sub if HC > 1 else F4
                w = [(s_mm, vln[idx - 1])] if idx >= 1 else []
                if idx >= 1:
                    w.append((s_v, vevac[idx - 1]))
                vtok = swv.load(lambda e, F4=F4, c0=sub * nhalf: e.dma_start(
                    wvb_sb[:, 0:nhalf * VFW],
                    wv_rs[F4][:, c0 * VFW:(c0 + nhalf) * VFW]), waits=w)
                c0 = sub * nhalf
                for ks in range(KSUB):
                    bank = [PS_O0, PS_O1, PS_X0, PS_X1][ks]
                    for ci in range(nhalf):
                        k = c0 + ci
                        mk = T(lambda e, k=k, ci=ci, ks=ks, bank=bank: e.matmul(
                            bank[:, 0:VFW], xnm_sb[:, k * MEGA + ks * 128:k * MEGA + ks * 128 + 128],
                            wvb_sb[:, ci * VFW:(ci + 1) * VFW],
                            start=(k == 0), stop=(k == HC - 1)),
                          waits=(vtok,) if ci == 0 else (),
                          mark=(ci == nhalf - 1))
                    if sub == (1 if HC > 1 else 0):
                        V(lambda e, F4=F4, ks=ks, bank=bank: e.scalar_tensor_tensor(
                            V_sb[:, ks * H + F4 * VFW:ks * H + (F4 + 1) * VFW],
                            bank[:, 0:VFW], 1.0,
                            bvbc_sb[:, F4 * VFW:(F4 + 1) * VFW], ALU.mult, ALU.add),
                          waits=((s_mm, mk), bvtok))
                vevac[idx] = CN[s_v]
                vln[idx] = CN[s_mm]
        vready = CN[s_v]
        R.barrier()
        exp_marks = {}
        pv_marks = {}
        oevac = {}
        for g in range(NQG):
            for h in range(NH):
                ghi = g * NH + h
                obank = [PS_O0, PS_O1][ghi % 2]
                for ks in range(KSUB):
                    kslot = M * KSUB + ks
                    sbank = SBANK[ks % 2]
                    spar = ks % 2
                    waits = [(s_v, vready)]
                    if (ghi * KSUB + ks) >= 2:
                        waits.append((s_sc, exp_marks[(ghi * KSUB + ks - 2) % 2_000_000]))
                    mk = T(lambda e, h=h, g=g, ks=ks, sbank=sbank: e.matmul(
                        sbank[:, 0:QGW], K_sb[:, h * MEGA + ks * 128:h * MEGA + ks * 128 + 128],
                        Q_sb[:, h * TQ + g * QGW:h * TQ + (g + 1) * QGW],
                        start=True, stop=True),
                      waits=waits, mark=True)
                    j = kslot - OWN0
                    trim = None
                    if j >= 0:
                        for qq in range(QSUBG):
                            if j == g * QSUBG + qq:
                                trim = V(lambda e, qq=qq, sbank=sbank: e.tensor_tensor(
                                    sbank[:, qq * 128:(qq + 1) * 128],
                                    sbank[:, qq * 128:(qq + 1) * 128],
                                    tri_sb[:, :], ALU.add), waits=((s_mm, mk),))
                    ew = [(s_mm, mk), (s_v, max(bias_ready, trim or 0))]
                    if (ghi * KSUB + ks) >= 2:
                        ew.append((s_mm, pv_marks.get((ghi * KSUB + ks - 2) % 2_000_000, 0)))
                    em = SC(lambda e, ks=ks, h=h, sbank=sbank, par=spar: e.activation(
                        PT_sb[par][:, 0:QGW], sbank[:, 0:QGW], AF.Exp,
                        bias=bias_sb[:, ks * NH + h:ks * NH + h + 1], scale=SCALE),
                        waits=ew)
                    exp_marks[(ghi * KSUB + ks) % 2_000_000] = em
                    memm = None
                    if j >= 0:
                        for qq in range(QSUBG):
                            if j > g * QSUBG + qq:
                                memm = V(lambda e, qq=qq, par=spar: e.memset(
                                    PT_sb[par][:, qq * 128:(qq + 1) * 128], 0.0),
                                    waits=((s_sc, em),))
                    if DBG and g == 0:
                        R.op("sync", lambda e, M=M, h=h, ks=ks, par=spar: e.dma_start(
                            dbgPT[M * 128:(M + 1) * 128,
                                  (h * KSUB + ks) * QGW:(h * KSUB + ks + 1) * QGW],
                            PT_sb[par][:, 0:QGW]),
                            waits=((s_sc, em),) + (((s_v, memm),) if memm else ()),
                            incs=((s_dm, 16),))
                    pvw = [(s_sc, em)]
                    if memm:
                        pvw.append((s_v, memm))
                    if DBG and g == 0:
                        pvw.append((s_dm, CN[s_dm]))
                    if ks == 0 and ghi >= 2:
                        pvw.append((s_v, oevac[ghi - 2]))
                    T(lambda e, ks=ks, h=h, par=spar, obank=obank: e.matmul(
                        obank[:, 0:QGW], V_sb[:, ks * H + h * 128:ks * H + h * 128 + 128],
                        PT_sb[par][:, 0:QGW], start=(ks == 0), stop=(ks == KSUB - 1)),
                      waits=pvw)
                    mk2 = T(lambda e, ks=ks, h=h, g=g, par=spar: e.matmul(
                        LBANK[g][0:NH, 0:QGW], indb_sb[:, h * NH:h * NH + NH],
                        PT_sb[par][:, 0:QGW],
                        start=(h == 0 and ks == 0),
                        stop=(h == NH - 1 and ks == KSUB - 1)),
                      mark=True)
                    pv_marks[(ghi * KSUB + ks) % 2_000_000] = mk2
                oevac[ghi] = V(lambda e, h=h, g=g, obank=obank: e.tensor_tensor(
                    O_sb[:, h * TQ + g * QGW:h * TQ + (g + 1) * QGW],
                    O_sb[:, h * TQ + g * QGW:h * TQ + (g + 1) * QGW],
                    obank[:, 0:QGW], ALU.add), waits=((s_mm, mk2),))
            V(lambda e, g=g: e.tensor_tensor(
                Lacc_sb[0:NH, g * QGW:(g + 1) * QGW],
                Lacc_sb[0:NH, g * QGW:(g + 1) * QGW],
                LBANK[g][0:NH, 0:QGW], ALU.add), waits=((s_mm, mk2),))
        if DBG:
            R.op("sync", lambda e, M=M: e.dma_start(
                dbgL[M * 16:M * 16 + 16, :], Lacc_sb[:, :]),
                waits=((s_v, CN[s_v]),), incs=((s_dm, 16),))
        R.barrier()
    esB.close()
    if PHASES < 3:
        R.op("sync", lambda e: e.nop(),
             waits=((s_res0, CN[s_res0]), (s_res1, CN[s_res1]), (s_dm, CN[s_dm]), (s_dmg, CN[s_dmg])))
        R.build()
        es.close()
        nc.finalize()
        nc._final_counts = {getattr(s, 'name', str(s)): v for s, v in CN.items()}
        return nc, c

    esC = ExitStack()
    csb = lambda n, shp, dt: esC.enter_context(nc.sbuf_tensor(n, list(shp), dt))
    Obf_sb = csb("Obf", [128, NH * TQ], BF16)
    wob_sb = [csb(f"wob{i}", [128, H], BF16) for i in range(2)]
    ytl_sb = [csb(f"ytl{i}", [128, QGW], F32) for i in range(2)]
    rtl_sb = [csb(f"rtl{i}", [128, TQ], F32) for i in range(2)]

    for g in range(NQG):
        V(lambda e, g=g: e.reciprocal(rL_sb[0:NH, g * QGW:(g + 1) * QGW],
                                      Lacc_sb[0:NH, g * QGW:(g + 1) * QGW]))
    vm = CN[s_v]
    srl2 = Dstream("srl2", "gpsimd")
    bcmk = {}
    for h in range(NH):
        for g in range(NQG):
            i = h * NQG + g
            slot = i % 2
            bank = [PS_X0, PS_X1][i % 2]
            w = [(s_v, vm)]
            if i >= 2:
                w.append((s_mm, bcmk[i - 2]))
            ltok = srl2.load(lambda e, h=h, g=g, slot=slot: e.dma_start(
                rl0_sb[0:1, slot * QGW:(slot + 1) * QGW],
                rL_sb[h:h + 1, g * QGW:(g + 1) * QGW]), waits=w)
            w2 = [ltok]
            if i >= 2:
                w2.append((s_v, CN[s_v]))
            tm = T(lambda e, slot=slot, bank=bank: e.matmul(
                bank[:, 0:QGW], ones1_sb[:, :],
                rl0_sb[0:1, slot * QGW:(slot + 1) * QGW],
                start=True, stop=True), waits=w2, mark=True)
            bcmk[i] = tm
            V(lambda e, h=h, g=g, bank=bank: e.tensor_tensor(
                Obf_sb[:, h * TQ + g * QGW:h * TQ + (g + 1) * QGW],
                O_sb[:, h * TQ + g * QGW:h * TQ + (g + 1) * QGW],
                bank[:, 0:QGW], ALU.mult), waits=((s_mm, tm),))
    R.barrier()
    yev = {}
    otmk = {}
    for m in range(HC):
        w = [(s_v, yev[(m - 2, NQG - 1)])] if m >= 2 else []
        wtok = swc.load(lambda e, m=m: e.dma_start(wob_sb[m % 2][:, 0:H], wo_cs[m]), waits=w)
        ow = [otmk[m - 2]] if m >= 2 else []
        rtok = stm.load(lambda e, m=m: e.dma_start(rtl_sb[m % 2][:, 0:TQ],
                                                   resid_hbm[m * 128:(m + 1) * 128, :]),
                        waits=[(s_res0, CN[s_res0]), (s_res1, CN[s_res1])] + w + ow)
        for gq in range(NQG):
            bank = SBANK[(m * NQG + gq) % 2]
            waits = [wtok]
            if (m * NQG + gq) >= 2:
                waits.append((s_v, yev[divmod(m * NQG + gq - 2, NQG)]))
            for k in range(HC):
                mk = T(lambda e, m=m, gq=gq, k=k, bank=bank: e.matmul(
                    bank[:, 0:QGW], wob_sb[m % 2][:, k * 128:(k + 1) * 128],
                    Obf_sb[:, k * TQ + gq * QGW:k * TQ + (gq + 1) * QGW],
                    start=(k == 0), stop=(k == HC - 1)),
                  waits=waits if k == 0 else (), mark=(k == HC - 1))
            V(lambda e, m=m, gq=gq, bank=bank: e.tensor_scalar_add(
                ytl_sb[gq % 2][:, 0:QGW], bank[:, 0:QGW], bo_sb[:, m:m + 1]),
              waits=((s_mm, mk), rtok))
            yev[(m, gq)] = V(lambda e, m=m, gq=gq: e.tensor_tensor(
                rtl_sb[m % 2][:, gq * QGW:(gq + 1) * QGW],
                rtl_sb[m % 2][:, gq * QGW:(gq + 1) * QGW],
                ytl_sb[gq % 2][:, 0:QGW], ALU.add))
        vm2 = CN[s_v]
        otmk[m] = (s_out[m % 2], R.op("sync", lambda e, m=m: e.dma_start(
            outT[m * 128:(m + 1) * 128, :], rtl_sb[m % 2][:, 0:TQ]),
            waits=((s_v, vm2),), incs=((s_out[m % 2], 16),)))
    R.op("sync", lambda e: e.nop(),
         waits=((s_out0, CN[s_out0]), (s_out1, CN[s_out1]), (s_dm, CN[s_dm])))

    R.build()
    esC.close()
    es.close()
    nc.finalize()
    nc._final_counts = {getattr(s, 'name', str(s)): v for s, v in CN.items()}
    return nc, c


def stage_inputs(cfg, x, Wqkv, bqkv, Wout, bout, gw, uw, dw):
    c = derive(cfg)
    H, NH, TQ, S, FFI = c["H"], c["NH"], c["TQ"], c["S"], c["FFI"]
    HC, VFW, VFG, FFB = c["HC"], c["VFW"], c["VFG"], c["FFB"]
    NKSLOT, OWN0 = c["NKSLOT"], c["OWN0"]
    f32 = np.float32
    ac = np.ascontiguousarray

    def colslab(W, nb):
        K = W.shape[1]
        return ac(W.reshape(nb, 128, K // 128, 128).transpose(0, 3, 2, 1)
                  .reshape(nb, 128, K)).astype(f32)

    Wq, Wk, Wv = Wqkv[:H], Wqkv[H:2 * H], Wqkv[2 * H:]
    shared = {
        "wq_cs": colslab(Wq, NH),
        "wk_cs": colslab(Wk, NH),
        "wv_rs": ac(Wv.reshape(VFG, VFW, HC, 128).transpose(0, 3, 2, 1)
                    .reshape(VFG, 128, HC * VFW)).astype(f32),
        "wo_cs": colslab(Wout, HC),
        "wg_cs": colslab(gw, FFB),
        "wu_cs": colslab(uw, FFB),
        "wd_cs": colslab(dw, HC),
        "wg_rows": ac(gw.reshape(FFB, 128, H)).astype(f32),
        "wu_rows": ac(uw.reshape(FFB, 128, H)).astype(f32),
        "wd_rows": ac(dw.reshape(HC, 128, FFI)).astype(f32),
        "bqkv_t": ac(bqkv.reshape(3 * HC, 128).T).astype(f32),
        "bout_t": ac(bout.reshape(HC, 128).T).astype(f32),
        "bv_bc": ac(np.broadcast_to(bqkv[2 * H:3 * H], (128, H))).astype(np.float32).astype(ml_dtypes.bfloat16),
        "tri_in": np.where(np.arange(128)[:, None] <= np.arange(128)[None, :],
                        np.float32(0.0), np.float32(NEG)),
    }
    ind = np.zeros((128, NH * NH), dtype=f32)
    for h in range(NH):
        ind[:, h * NH + h] = 1.0
    shared["ind_b"] = ind.astype(ml_dtypes.bfloat16)
    e32 = np.zeros((64, 8 * 128), dtype=f32)
    for cb in range(8):
        for p in range(128):
            kb = cb * 4 + p // 32
            if kb < 32:
                e32[kb, cb * 128 + p] = 1.0
                e32[32 + kb, cb * 128 + p] = 1.0
    shared["e32_in"] = e32
    shared["id_in"] = np.eye(128, dtype=f32)

    NCHUNK = S // TQ
    in_maps = []
    for core in range(2 * NCHUNK):
        b, j = core // NCHUNK, core % NCHUNK
        xb = np.asarray(x[b], dtype=f32)
        plen = j * TQ
        xsT = np.zeros((H, S), dtype=f32)
        if plen:
            xsT[:, :plen] = xb[:plen].T
        xsT[:, S - TQ:] = xb[j * TQ:(j + 1) * TQ].T
        flag = np.zeros((128, NKSLOT), dtype=f32)
        flag[:, plen // 128:OWN0] = NEG
        m = dict(shared)
        m["xs"] = ac(xsT)
        m["flag_in"] = flag
        in_maps.append(m)
    return in_maps


_NC_CACHE = {}


def kernel(L0_attn_x, L0_attn_W_qkv, L0_attn_b_qkv, L0_attn_W_out, L0_attn_b_out,
           L0_ffn_gate_w, L0_ffn_up_w, L0_ffn_down_w):
    cfg = FULL_CFG
    c = derive(cfg)
    x = np.asarray(L0_attn_x, dtype=np.float32)
    in_maps = stage_inputs(cfg, x,
                           np.asarray(L0_attn_W_qkv, dtype=np.float32),
                           np.asarray(L0_attn_b_qkv, dtype=np.float32),
                           np.asarray(L0_attn_W_out, dtype=np.float32),
                           np.asarray(L0_attn_b_out, dtype=np.float32),
                           np.asarray(L0_ffn_gate_w, dtype=np.float32),
                           np.asarray(L0_ffn_up_w, dtype=np.float32),
                           np.asarray(L0_ffn_down_w, dtype=np.float32))
    key = tuple(sorted(cfg.items()))
    if key not in _NC_CACHE:
        _NC_CACHE[key] = build_nc(cfg)
    nc, _ = _NC_CACHE[key]
    res = run_bass_kernel_spmd(nc, in_maps, core_ids=list(range(8)))
    TQ = c["TQ"]
    out = np.empty((2, c["S"], c["H"]), dtype=np.float32)
    for core in range(8):
        b, j = core // 4, core % 4
        out[b, j * TQ:(j + 1) * TQ, :] = np.asarray(res.results[core]["outT"]).T
    return out


# version 07
# speedup vs baseline: 1.2503x; 1.2503x over previous
import numpy as np
import ml_dtypes
from contextlib import ExitStack

import concourse.bass as bass
import concourse.mybir as mybir
from concourse.bass_utils import run_bass_kernel_spmd

F32 = mybir.dt.float32
BF16 = mybir.dt.bfloat16
FP8 = mybir.dt.float8e4
AF = mybir.ActivationFunctionType
ALU = mybir.AluOpType
AX = mybir.AxisListType

CUSH = 70.0
EPS = 1e-5
NEG = -1e30

FULL_CFG = dict(H=2048, NH=16, TQ=1024, S=4096, FFI=8192)


def derive(cfg):
    c = dict(cfg)
    c["HC"] = c["H"] // 128
    c["SCALE"] = float(1.0 / np.sqrt(128.0))
    c["QGW"] = min(512, c["TQ"])
    c["NQG"] = c["TQ"] // c["QGW"]
    c["QSUBG"] = c["QGW"] // 128
    c["MEGA"] = min(512, c["S"])
    c["NMEGA"] = c["S"] // c["MEGA"]
    c["KSUB"] = c["MEGA"] // 128
    c["NKSLOT"] = c["S"] // 128
    c["NQS"] = c["TQ"] // 128
    c["OWN0"] = c["NKSLOT"] - c["NQS"]
    c["VFW"] = min(512, c["H"])
    c["VFG"] = c["H"] // c["VFW"]
    c["FFB"] = c["FFI"] // 128
    c["NCH"] = max(2, c["FFI"] // 2048)
    c["FFHALF"] = c["FFI"] // c["NCH"]
    c["DQ"] = min(2048, c["FFHALF"])
    c["NDQ"] = c["FFHALF"] // c["DQ"]
    return c


class Rec:
    ENGS = ("sync", "gpsimd", "tensor", "vector", "scalar")

    def __init__(self, nc):
        self.nc = nc
        self.stages = [[]]
        self.CN = {}
        self.es = ExitStack()

    def sem(self, name):
        s = self.es.enter_context(self.nc.semaphore(name))
        self.CN[s] = 0
        return s

    def barrier(self):
        if self.stages[-1]:
            self.stages.append([])

    def op(self, eng, fn, waits=(), incs=()):
        rw = tuple((s, int(v)) for (s, v) in waits)
        ri = []
        out = None
        for (s, k) in incs:
            self.CN[s] += k
            ri.append((s, k))
            if out is None:
                out = self.CN[s]
        self.stages[-1].append((eng, fn, rw, tuple(ri)))
        return out

    def build(self):
        for stage in self.stages:
            if not stage:
                continue
            with self.nc.Block() as blk:
                for engname in self.ENGS:
                    ops = [o for o in stage if o[0] == engname]
                    if not ops:
                        continue
                    def body(eng, ops=ops):
                        for (_, fn, waits, incs) in ops:
                            for (s, v) in waits:
                                eng.wait_ge(s, v)
                            inst = fn(eng)
                            for (s, k) in incs:
                                inst.then_inc(s, k)
                    getattr(blk, engname)(body)


def build_nc(cfg):
    c = derive(cfg)
    H, NH, TQ, S, FFI = c["H"], c["NH"], c["TQ"], c["S"], c["FFI"]
    HC, SCALE = c["HC"], c["SCALE"]
    QGW, NQG, QSUBG = c["QGW"], c["NQG"], c["QSUBG"]
    MEGA, NMEGA, KSUB = c["MEGA"], c["NMEGA"], c["KSUB"]
    NKSLOT, OWN0 = c["NKSLOT"], c["OWN0"]
    VFW, VFG = c["VFW"], c["VFG"]
    FFB, FFHALF, DQ, NDQ = c["FFB"], c["FFHALF"], c["DQ"], c["NDQ"]
    NCH = c["NCH"]
    HINV = float(1.0 / H)
    NHB = FFHALF // 128
    SEGW = max(TQ, MEGA)

    PHASES = int(cfg.get("phases", 3))
    FFNLITE = int(cfg.get("ffnlite", 3))
    nc = bass.Bass("TRN2", target_bir_lowering=False)
    R = Rec(nc)

    def din(name, shape, dt=F32):
        return nc.dram_tensor(name, list(shape), dt, kind="ExternalInput").ap()

    xs = din("xs", (H, S))
    wq_cs = din("wq_cs", (NH, 128, H))
    wk_cs = din("wk_cs", (NH, 128, H))
    wv_rs = din("wv_rs", (VFG, 128, HC * VFW))
    wo_cs = din("wo_cs", (HC, 128, H))
    wg_rows = din("wg_rows", (FFB, 128, H))
    wu_rows = din("wu_rows", (FFB, 128, H))
    wd_rows = din("wd_rows", (HC, 128, FFI))
    bqkv_t = din("bqkv_t", (128, 3 * HC))
    bout_t = din("bout_t", (128, HC))
    bv_bc = din("bv_bc", (128, H), BF16)
    tri_c = din("tri_in", (128, 128))
    flag_c = din("flag_in", (128, NKSLOT))
    ind_b = din("ind_b", (128, NH * NH), BF16)
    id_in = din("id_in", (128, 128))

    dacc_hbm = nc.dram_tensor("dacc_hbm", [H, TQ], F32, kind="Internal").ap()
    resid_hbm = nc.dram_tensor("resid_hbm", [H, TQ], F32, kind="Internal").ap()
    outT = nc.dram_tensor("outT", [H, TQ], F32, kind="ExternalOutput").ap()
    DBG = bool(cfg.get("dbg"))
    if DBG:
        dbgK = nc.dram_tensor("dbgK", [NMEGA * 128, NH * MEGA], BF16, kind="ExternalOutput").ap()
        dbgL = nc.dram_tensor("dbgL", [NMEGA * 16, NQG * QGW], F32, kind="ExternalOutput").ap()
        dbgPT = nc.dram_tensor("dbgPT", [NMEGA * 128, NH * KSUB * QGW], BF16, kind="ExternalOutput").ap()

    es = R.es
    sbuf = lambda n, shp, dt: es.enter_context(nc.sbuf_tensor(n, list(shp), dt))
    mkps = lambda n: es.enter_context(nc.psum_tensor(n, [128, 512], F32))

    s_dm = R.sem("s_dm")
    s_dmg = R.sem("s_dmg")
    s_res0 = R.sem("s_res0")
    s_res1 = R.sem("s_res1")
    s_res = [s_res0, s_res1]
    s_out0 = R.sem("s_out0")
    s_out1 = R.sem("s_out1")
    s_out = [s_out0, s_out1]
    s_mm = R.sem("s_mm")
    s_v = R.sem("s_v")
    s_sc = R.sem("s_sc")
    s_rl = R.sem("s_rl")

    CN = R.CN

    class Dstream:
        def __init__(self, name, queue, npar=2):
            self.sems = [R.sem(f"{name}{i}") for i in range(npar)]
            self.queue = queue
            self.n = 0
        def load(self, fn, waits=()):
            s = self.sems[self.n % len(self.sems)]
            self.n += 1
            cnt = R.op(self.queue, fn, waits=waits, incs=((s, 16),))
            return (s, cnt)

    stx = Dstream("stx", "sync")
    strow = Dstream("strow", "sync")
    stm = Dstream("stm", "sync")
    swc = Dstream("swc", "gpsimd")
    swv = Dstream("swv", "gpsimd")

    def V(fn, waits=()):
        return R.op("vector", fn, waits=tuple(waits) + ((s_v, CN[s_v]),), incs=((s_v, 1),))

    def SC(fn, waits=()):
        return R.op("scalar", fn, waits=tuple(waits) + ((s_sc, CN[s_sc]),), incs=((s_sc, 1),))

    def T(fn, waits=(), mark=False):
        return R.op("tensor", fn, waits=waits, incs=((s_mm, 1),) if mark else ())

    Q_sb = sbuf("Q", [128, NH * TQ], BF16)
    bq_sb = sbuf("bq", [128, 3 * HC], F32)
    bo_sb = sbuf("bo", [128, HC], F32)
    tri_sb = sbuf("tri", [128, 128], F32)
    flag_sb = sbuf("flag", [128, NKSLOT], F32)
    indb_sb = sbuf("indb", [128, NH * NH], BF16)
    ones1_sb = sbuf("ones1", [1, 128], F32)
    onesc_sb = sbuf("onesc", [128, 1], F32)
    stat_sb = sbuf("stat", [32, 64], F32)
    rowt_sb = sbuf("rowt", [32, 32], F32)
    biasrow_sb = sbuf("biasrow", [1, 4 * 32], F32)
    bias_sb = sbuf("biasx", [128, KSUB * NH], F32)
    facbc_sb = sbuf("facbc", [128, NH], F32)
    rl0_sb = sbuf("rl0", [1, 2 * 512], F32)
    rL_sb = sbuf("rL", [16, NQG * QGW], F32)
    Lacc_sb = sbuf("Lacc", [16, NQG * QGW], F32)
    id128_sb = sbuf("id128", [128, 128], F32)
    tqT_sb = sbuf("tqT", [128, 512], F32)
    tq8_sb = sbuf("tq8", [128, 512], FP8)
    tqb_sb = sbuf("tqb", [128, 512], BF16)
    bmt_sb = sbuf("bmt", [128, 16], F32)
    rsbt_sb = sbuf("rsbt", [128, 16], F32)
    s2bt_sb = sbuf("s2bt", [128, 16], F32)

    C_QMAX, C_KMAX, C_SB, C_SBN, C_FAC, C_T0, C_T1, C_T2 = range(8)
    stc = lambda col: stat_sb[0:NH, col:col + 1]

    PS_S0 = mkps("ps_s0"); PS_S1 = mkps("ps_s1")
    PS_O0 = mkps("ps_o0"); PS_O1 = mkps("ps_o1")
    PS_L0 = mkps("ps_l0"); PS_L1 = mkps("ps_l1")
    PS_X0 = mkps("ps_x0"); PS_X1 = mkps("ps_x1")
    SBANK = [PS_S0, PS_S1]
    LBANK = [PS_L0, PS_L1]
    XBANK2 = [PS_X0, PS_X1]

    for apv, dst in ((bqkv_t, bq_sb), (bout_t, bo_sb), (tri_c, tri_sb),
                     (flag_c, flag_sb), (id_in, id128_sb)):
        R.op("sync", lambda e, apv=apv, dst=dst: e.dma_start(dst[:, :], apv),
             incs=((s_dm, 16),))
    R.op("gpsimd", lambda e: e.dma_start(indb_sb[:, :], ind_b), incs=((s_dmg, 16),))
    V(lambda e: e.memset(ones1_sb[:, :], 1.0))
    V(lambda e: e.memset(onesc_sb[:, :], 1.0))
    V(lambda e: e.memset(stat_sb[:, :], 0.0))
    V(lambda e: e.memset(rowt_sb[:, :], 0.0))
    V(lambda e: e.memset(biasrow_sb[:, :], 0.0))
    cdm = CN[s_dm]
    cdw = CN[s_dmg]
    R.barrier()
    T(lambda e: e.nop(), waits=((s_dm, cdm), (s_dmg, cdw)))
    V(lambda e: e.nop(), waits=((s_dm, cdm), (s_dmg, cdw)))
    SC(lambda e: e.nop(), waits=((s_dm, cdm), (s_dmg, cdw)))

    def emit_quantT(src_f32_ap, dst_bf16_ap, width=512):
        nj = width // 32
        V(lambda e: e.transpose(tqT_sb[:, 0:width], src_f32_ap))
        V(lambda e: e.tensor_reduce(
            bmt_sb[:, 0:nj], tqT_sb[:, 0:width].rearrange("p (j b) -> p j b", j=nj),
            AX.X, ALU.max, apply_absolute_value=True))
        V(lambda e: e.tensor_scalar(s2bt_sb[:, 0:nj], bmt_sb[:, 0:nj],
                                    1.0 / 448.0, 1e-30, ALU.mult, ALU.max))
        V(lambda e: e.reciprocal(rsbt_sb[:, 0:nj], s2bt_sb[:, 0:nj]))
        V(lambda e: e.tensor_scalar_mul(rsbt_sb[:, 0:nj], rsbt_sb[:, 0:nj], 0.5))
        V(lambda e: e.tensor_scalar_mul(s2bt_sb[:, 0:nj], s2bt_sb[:, 0:nj], 2.0))
        V(lambda e: e.tensor_tensor(
            tq8_sb[:, 0:width], tqT_sb[:, 0:width],
            rsbt_sb[:, 0:nj].broadcast_to([128, nj, 32]), ALU.mult))
        V(lambda e: e.tensor_tensor(
            tqb_sb[:, 0:width], tq8_sb[:, 0:width],
            s2bt_sb[:, 0:nj].broadcast_to([128, nj, 32]), ALU.mult))
        V(lambda e: e.transpose(dst_bf16_ap, tqb_sb[:, 0:width]))

    def emit_wquant(K, rowbuf, dqbuf, extra_waits=()):
        nkb = K // 32
        V(lambda e: e.tensor_reduce(
            bmw_sb[:, 0:nkb], rowbuf[:, 0:K].rearrange("p (b k) -> p b k", k=32),
            AX.X, ALU.max, apply_absolute_value=True), waits=extra_waits)
        V(lambda e: e.tensor_scalar(s2w_sb[:, 0:nkb], bmw_sb[:, 0:nkb],
                                    1.0 / 448.0, 1e-30, ALU.mult, ALU.max))
        V(lambda e: e.reciprocal(rsw_sb[:, 0:nkb], s2w_sb[:, 0:nkb]))
        V(lambda e: e.tensor_scalar_mul(s2w_sb[:, 0:nkb], s2w_sb[:, 0:nkb], 2.0))
        V(lambda e: e.scalar_tensor_tensor(
            q8w_sb[:, 0:K], rowbuf[:, 0:K], 0.5,
            rsw_sb[:, 0:nkb].broadcast_to([128, nkb, 32]), ALU.mult, ALU.mult))
        vm = V(lambda e: e.tensor_tensor(
            dqr_sb[:, 0:K], q8w_sb[:, 0:K],
            s2w_sb[:, 0:nkb].broadcast_to([128, nkb, 32]), ALU.mult))
        evm = {}
        for cc2 in range(K // 128):
            bank = XBANK2[cc2 % 2]
            w = [(s_v, vm)] if cc2 < 2 else [(s_v, evm[cc2 - 2])]
            tm = T(lambda e, cc2=cc2, bank=bank: e.transpose(
                bank[:, 0:128], dqr_sb[:, cc2 * 128:(cc2 + 1) * 128], id128_sb[:, :]),
                waits=w, mark=True)
            evm[cc2] = V(lambda e, cc2=cc2, bank=bank: e.tensor_copy(
                dqbuf[:, cc2 * 128:(cc2 + 1) * 128], bank[:, 0:128]),
                waits=((s_mm, tm),))
        return CN[s_v]

    def emit_ln(tok0, ntok, xst, xsq, xn_write, xnf_consumer,
                sumbanks, bcbanks, srow_sb, SEG):
        ngrp = max(1, ntok // 512)
        gw = ntok // ngrp
        sq_marks = {}
        mm_marks = {}
        dtok = {}
        for cc in range(HC):
            w = [(s_mm, mm_marks[cc - 2])] if cc >= 2 else []
            dtok[cc] = stx.load(lambda e, cc=cc: e.dma_start(
                xst[cc % 2][:, 0:ntok], xs[cc * 128:(cc + 1) * 128, tok0:tok0 + ntok]),
               waits=w)
            sq_marks[cc] = SC(lambda e, cc=cc: e.activation(
                xsq[cc % 2][:, 0:ntok], xst[cc % 2][:, 0:ntok], AF.Square),
                waits=[dtok[cc]] + ([(s_mm, mm_marks[cc - 2])] if cc >= 2 else []))
            for gq in range(ngrp):
                T(lambda e, cc=cc, gq=gq: e.matmul(
                    sumbanks[gq][0][0:1, 0:gw], onesc_sb[:, :],
                    xst[cc % 2][:, gq * gw:(gq + 1) * gw],
                    start=(cc == 0), stop=(cc == HC - 1)),
                  waits=(dtok[cc],))
            for gq in range(ngrp):
                mk = T(lambda e, cc=cc, gq=gq: e.matmul(
                    sumbanks[gq][1][0:1, 0:gw], onesc_sb[:, :],
                    xsq[cc % 2][:, gq * gw:(gq + 1) * gw],
                    start=(cc == 0), stop=(cc == HC - 1)),
                  waits=((s_sc, sq_marks[cc]),), mark=(gq == ngrp - 1))
            mm_marks[cc] = CN[s_mm]
        R.barrier()
        smw = [(s_mm, CN[s_mm])]
        for gq in range(ngrp):
            sgq = slice(gq * gw, (gq + 1) * gw)
            V(lambda e, sgq=sgq, gq=gq: e.tensor_scalar_mul(
                srow_sb[0:1, sgq], sumbanks[gq][0][0:1, 0:gw], HINV), waits=smw)
            smw = ()
            V(lambda e, sgq=sgq, gq=gq: e.tensor_scalar_mul(
                srow_sb[0:1, 2 * SEG + gq * gw:2 * SEG + (gq + 1) * gw],
                sumbanks[gq][1][0:1, 0:gw], HINV))
        V(lambda e: e.tensor_mul(srow_sb[0:1, SEG:SEG + ntok],
                                 srow_sb[0:1, 0:ntok], srow_sb[0:1, 0:ntok]))
        V(lambda e: e.tensor_sub(srow_sb[0:1, 2 * SEG:2 * SEG + ntok],
                                 srow_sb[0:1, 2 * SEG:2 * SEG + ntok], srow_sb[0:1, SEG:SEG + ntok]))
        vm = V(lambda e: e.tensor_scalar_add(srow_sb[0:1, 2 * SEG:2 * SEG + ntok], srow_sb[0:1, 2 * SEG:2 * SEG + ntok], EPS))
        SC(lambda e: e.activation(srow_sb[0:1, 2 * SEG:2 * SEG + ntok], srow_sb[0:1, 2 * SEG:2 * SEG + ntok], AF.Ln),
           waits=((s_v, vm),))
        scm = SC(lambda e: e.activation(srow_sb[0:1, 2 * SEG:2 * SEG + ntok], srow_sb[0:1, 2 * SEG:2 * SEG + ntok],
                                        AF.Exp, scale=-0.5))
        vm = V(lambda e: e.scalar_tensor_tensor(
            srow_sb[0:1, 0:ntok], srow_sb[0:1, 0:ntok], -1.0, srow_sb[0:1, 2 * SEG:2 * SEG + ntok],
            ALU.mult, ALU.mult), waits=((s_sc, scm),))
        for gq in range(ngrp):
            br, bs = bcbanks[gq]
            T(lambda e, gq=gq, br=br: e.matmul(
                br[:, 0:gw], ones1_sb[:, :], srow_sb[0:1, 2 * SEG + gq * gw:2 * SEG + (gq + 1) * gw],
                start=True, stop=True), waits=((s_v, vm),))
            T(lambda e, gq=gq, bs=bs: e.matmul(
                bs[:, 0:gw], ones1_sb[:, :], srow_sb[0:1, gq * gw:(gq + 1) * gw],
                start=True, stop=True), mark=(gq == ngrp - 1))
        bcm = CN[s_mm]
        v_marks = {}
        for cc in range(HC):
            w = [(s_v, v_marks[cc - 2])] if cc >= 2 else []
            dtok[cc] = stx.load(lambda e, cc=cc: e.dma_start(
                xst[cc % 2][:, 0:ntok], xs[cc * 128:(cc + 1) * 128, tok0:tok0 + ntok]),
               waits=w)
            first = [dtok[cc], (s_mm, bcm)]
            for gq in range(ngrp):
                br, bs = bcbanks[gq]
                sgq = slice(gq * gw, (gq + 1) * gw)
                V(lambda e, cc=cc, sgq=sgq, br=br: e.tensor_tensor(
                    xst[cc % 2][:, sgq], xst[cc % 2][:, sgq], br[:, 0:gw], ALU.mult),
                  waits=first)
                first = ()
                V(lambda e, cc=cc, sgq=sgq, bs=bs: e.tensor_tensor(
                    xst[cc % 2][:, sgq], xst[cc % 2][:, sgq], bs[:, 0:gw], ALU.add))
            wdst = xn_write(cc)
            if wdst is not None:
                V(lambda e, cc=cc, wdst=wdst: e.tensor_copy(wdst, xst[cc % 2][:, 0:ntok]))
            if xnf_consumer is not None:
                xnf_consumer(cc, xst[cc % 2])
            v_marks[cc] = CN[s_v]
        return CN[s_v]


    def _early_exit(stacks):
        R.op("sync", lambda e: e.nop(),
             waits=((s_res0, CN[s_res0]), (s_res1, CN[s_res1]),
                    (s_dm, CN[s_dm]), (s_dmg, CN[s_dmg])))
        R.build()
        for st in stacks:
            st.close()
        es.close()
        nc.finalize()
        nc._final_counts = {getattr(s, 'name', str(s)): v for s, v in CN.items()}
        return nc, c

    esA = ExitStack()
    asb = lambda n, shp, dt: esA.enter_context(nc.sbuf_tensor(n, list(shp), dt))
    xdq_sb = asb("xdq", [128, HC * TQ], BF16)
    xstA = [asb(f"xstA{i}", [128, TQ], F32) for i in range(2)]
    srowA_sb = asb("srowA", [1, 3 * TQ], F32)

    esA1 = ExitStack()
    xn_sb = esA1.enter_context(nc.sbuf_tensor("xn", [128, HC * TQ], BF16))
    xsqA = [esA1.enter_context(nc.sbuf_tensor(f"xsqA{i}", [128, TQ], F32)) for i in range(2)]
    qsqb_sb = [esA1.enter_context(nc.sbuf_tensor(f"qsqb{i}", [128, TQ], BF16)) for i in range(2)]
    wqb_sb = [esA1.enter_context(nc.sbuf_tensor(f"wqb{i}", [128, H], BF16)) for i in range(2)]

    def a_consume(cc, buf):
        w5 = min(512, TQ)
        for gq in range(TQ // w5):
            emit_quantT(buf[:, gq * w5:(gq + 1) * w5],
                        xdq_sb[:, cc * TQ + gq * w5:cc * TQ + (gq + 1) * w5], width=w5)

    emit_ln(S - TQ, TQ, xstA, xsqA,
            lambda cc: xn_sb[:, cc * TQ:(cc + 1) * TQ],
            a_consume,
            sumbanks=[(PS_X0, PS_X1), (PS_S0, PS_S1)],
            bcbanks=[(PS_O0, PS_O1), (PS_X0, PS_X1)], srow_sb=srowA_sb, SEG=TQ)
    R.barrier()
    if PHASES == 11:
        return _early_exit([esA1, esA])

    evac_marks = {}
    for h in range(NH):
        w = [(s_v, evac_marks[(h - 2, NQG - 1)])] if h >= 2 else []
        wtok = swc.load(lambda e, h=h: e.dma_start(wqb_sb[h % 2][:, 0:H], wq_cs[h]), waits=w)
        waits = [wtok]
        if h >= 1:
            waits.append((s_v, evac_marks[(h - 1, NQG - 1)]))
        mks = {}
        for k in range(HC):
            for gq in range(NQG):
                bank = SBANK[gq]
                mks[gq] = T(lambda e, h=h, gq=gq, k=k, bank=bank: e.matmul(
                    bank[:, 0:QGW], wqb_sb[h % 2][:, k * 128:(k + 1) * 128],
                    xn_sb[:, k * TQ + gq * QGW:k * TQ + (gq + 1) * QGW],
                    start=(k == 0), stop=(k == HC - 1)),
                  waits=waits if (k == 0 and gq == 0) else (), mark=(k == HC - 1))
        for gq in range(NQG):
            bank = SBANK[gq]
            evac_marks[(h, gq)] = V(lambda e, h=h, gq=gq, bank=bank: e.tensor_scalar_add(
                Q_sb[:, h * TQ + gq * QGW:h * TQ + (gq + 1) * QGW],
                bank[:, 0:QGW], bq_sb[:, h:h + 1]), waits=((s_mm, mks[gq]),))
    R.barrier()

    sqm = {}
    qtm = {}
    for h in range(NH):
        sqm[h] = V(lambda e, h=h: e.tensor_mul(
            qsqb_sb[h % 2][:, 0:TQ], Q_sb[:, h * TQ:(h + 1) * TQ], Q_sb[:, h * TQ:(h + 1) * TQ]),
            waits=[(s_mm, qtm[h - 2])] if h >= 2 else ())
        for gq in range(NQG):
            mk = T(lambda e, h=h, gq=gq: e.matmul(
                [PS_X0, PS_X1][gq][0:NH, 0:QGW], indb_sb[:, h * NH:h * NH + NH],
                qsqb_sb[h % 2][:, gq * QGW:(gq + 1) * QGW],
                start=(h == 0), stop=(h == NH - 1)),
              waits=((s_v, sqm[h]),), mark=True)
        qtm[h] = mk
    V(lambda e: e.reduce_max(stc(C_T0), PS_X0[0:NH, 0:QGW], AX.X), waits=((s_mm, mk),))
    if NQG > 1:
        V(lambda e: e.reduce_max(stc(C_T1), PS_X1[0:NH, 0:QGW], AX.X))
        V(lambda e: e.tensor_max(stc(C_T0), stc(C_T0), stc(C_T1)))
    vm = CN[s_v]
    SC(lambda e: e.activation(stc(C_T0), stc(C_T0), AF.Ln), waits=((s_v, vm),))
    SC(lambda e: e.activation(stc(C_QMAX), stc(C_T0), AF.Exp, scale=0.5))
    R.barrier()
    if PHASES == 12:
        return _early_exit([esA1, esA])
    esA1.close()

    esA2 = ExitStack()
    hh_sb = esA2.enter_context(nc.sbuf_tensor("hh", [128, NHB * TQ], BF16))
    q8w_sb = esA2.enter_context(nc.sbuf_tensor("q8w", [128, 2048], FP8))
    bmw_sb = esA2.enter_context(nc.sbuf_tensor("bmw", [128, 128], F32))
    rsw_sb = esA2.enter_context(nc.sbuf_tensor("rsw", [128, 128], F32))
    s2w_sb = esA2.enter_context(nc.sbuf_tensor("s2w", [128, 128], F32))
    dqr_sb = esA2.enter_context(nc.sbuf_tensor("dqr", [128, 2048], F32))
    wrow_sb = [esA2.enter_context(nc.sbuf_tensor(f"wrow{i}", [128, DQ], F32)) for i in range(2)]
    dqw_sb = [esA2.enter_context(nc.sbuf_tensor(f"dqw{i}", [128, DQ], BF16)) for i in range(2)]
    sg_sb = esA2.enter_context(nc.sbuf_tensor("sg", [128, 2 * QGW], F32))
    hf_sb = esA2.enter_context(nc.sbuf_tensor("hf", [128, QGW], F32))
    dtl_sb = [esA2.enter_context(nc.sbuf_tensor(f"dtl{i}", [128, TQ], F32)) for i in range(2)]

    for half in range(NCH):
        use_marks = {}
        gate_mm = {}
        last_mm = {}
        for j in range(2 * NHB):
            ib = half * NHB + j // 2
            par = j % 2
            w = [(s_v, use_marks[j - 2])] if j >= 2 else []
            rtok = strow.load(lambda e, ib=ib, par=par, j=j: e.dma_start(
                wrow_sb[par][:, 0:H], (wg_rows if j % 2 == 0 else wu_rows)[ib]), waits=w)
            xw = [rtok]
            if j >= 2:
                xw.append((s_mm, last_mm[j - 2]))
            emit_wquant(H, wrow_sb[par], dqw_sb[par], extra_waits=tuple(xw))
            qm = CN[s_v]
            if FFNLITE < 2:
                last_mm[j] = CN[s_mm]
                use_marks[j] = CN[s_v]
                continue
            waits = [(s_v, qm), (s_sc, CN[s_sc])]
            for k in range(HC):
                for gq in range(NQG):
                    bank = [SBANK, [PS_O0, PS_O1]][par][gq]
                    mk = T(lambda e, par=par, gq=gq, k=k, bank=bank: e.matmul(
                        bank[:, 0:QGW], dqw_sb[par][:, k * 128:(k + 1) * 128],
                        xdq_sb[:, k * TQ + gq * QGW:k * TQ + (gq + 1) * QGW],
                        start=(k == 0), stop=(k == HC - 1)),
                      waits=waits if (k == 0 and gq == 0) else (), mark=(k == HC - 1))
                    if k == HC - 1:
                        if j % 2 == 0:
                            gate_mm[(ib, gq)] = mk
                        last_mm[j] = mk
            for gq in range(NQG):
                if j % 2 == 0 or FFNLITE < 3:
                    continue
                if True:
                    gb = SBANK[gq]
                    ub = [PS_O0, PS_O1][gq]
                    scm = SC(lambda e, gb=gb, gq=gq: e.activation(
                        sg_sb[:, gq * QGW:(gq + 1) * QGW], gb[:, 0:QGW], AF.Silu),
                             waits=((s_mm, gate_mm[(ib, gq)]),))
                    V(lambda e, ub=ub, gq=gq: e.tensor_tensor(
                        hf_sb[:, 0:QGW], sg_sb[:, gq * QGW:(gq + 1) * QGW],
                        ub[:, 0:QGW], ALU.mult),
                      waits=((s_sc, scm), (s_mm, mk)))
                    wdt = min(512, QGW)
                    for sg2 in range(QGW // wdt):
                        loc = (ib % NHB) * TQ + gq * QGW + sg2 * wdt
                        emit_quantT(hf_sb[:, sg2 * wdt:(sg2 + 1) * wdt],
                                    hh_sb[:, loc:loc + wdt], width=wdt)
            use_marks[j] = CN[s_v]
        R.barrier()
        if PHASES == 13:
            return _early_exit([esA2, esA])
        duse = {}
        dmms = {}
        evmk = {}
        stmk = {}
        mk_gq = {}
        for m in range(HC):
            for qtr in range(NDQ):
                jj = m * NDQ + qtr
                par = jj % 2
                w = [(s_v, duse[jj - 2])] if jj >= 2 else []
                k0 = half * FFHALF + qtr * DQ
                rtok = strow.load(lambda e, m=m, k0=k0, par=par: e.dma_start(
                    wrow_sb[par][:, 0:DQ], wd_rows[m][:, k0:k0 + DQ]), waits=w)
                xw = [rtok]
                if jj >= 2:
                    xw.append((s_mm, dmms[jj - 2]))
                emit_wquant(DQ, wrow_sb[par], dqw_sb[par], extra_waits=tuple(xw))
                qm = CN[s_v]
                waits = [(s_v, qm)]
                if qtr == 0:
                    prev = m - (2 if NQG == 1 else 1)
                    if prev >= 0:
                        waits.append((s_v, evmk[prev]))
                nk = DQ // 128
                for k in range(nk):
                    kg = qtr * nk + k
                    for gq in range(NQG):
                        bank = SBANK[(m * NQG + gq) % 2]
                        mk = T(lambda e, par=par, gq=gq, k=k, kg=kg, bank=bank: e.matmul(
                            bank[:, 0:QGW], dqw_sb[par][:, k * 128:(k + 1) * 128],
                            hh_sb[:, kg * TQ + gq * QGW:kg * TQ + (gq + 1) * QGW],
                            start=(kg == 0), stop=(kg == NHB - 1)),
                          waits=waits if (k == 0 and gq == 0) else (), mark=(kg == NHB - 1))
                        mk_gq[gq] = mk
                duse[jj] = CN[s_v]
                dmms[jj] = CN[s_mm]
            if half > 0:
                dtok1 = stm.load(lambda e, m=m: e.dma_start(
                    dtl_sb[m % 2][:, 0:TQ], dacc_hbm[m * 128:(m + 1) * 128, :]),
                   waits=((s_res0, CN[s_res0]), (s_res1, CN[s_res1])) if m <= 1 else
                         [stmk[m - 2]])
            if half == NCH - 1:
                dtok2 = stm.load(lambda e, m=m: e.dma_start(
                    xstA[m % 2][:, 0:TQ], xs[m * 128:(m + 1) * 128, S - TQ:S]))
            for gq in range(NQG):
                bank = SBANK[(m * NQG + gq) % 2]
                sw = [stmk[m - 2]] if m >= 2 else []
                if half == 0:
                    V(lambda e, m=m, gq=gq, bank=bank: e.tensor_copy(
                        dtl_sb[m % 2][:, gq * QGW:(gq + 1) * QGW], bank[:, 0:QGW]),
                      waits=[(s_mm, mk_gq[gq])] + sw)
                else:
                    V(lambda e, m=m, gq=gq, bank=bank: e.tensor_tensor(
                        dtl_sb[m % 2][:, gq * QGW:(gq + 1) * QGW],
                        dtl_sb[m % 2][:, gq * QGW:(gq + 1) * QGW],
                        bank[:, 0:QGW], ALU.add),
                      waits=[(s_mm, mk_gq[gq]), dtok1] + sw)
                    if half == NCH - 1:
                        V(lambda e, m=m, gq=gq: e.tensor_tensor(
                            dtl_sb[m % 2][:, gq * QGW:(gq + 1) * QGW],
                            dtl_sb[m % 2][:, gq * QGW:(gq + 1) * QGW],
                            xstA[m % 2][:, gq * QGW:(gq + 1) * QGW], ALU.add),
                          waits=(dtok2,))
            evmk[m] = vm2 = CN[s_v]
            dst = dacc_hbm if half < NCH - 1 else resid_hbm
            stmk[m] = (s_res[m % 2], R.op("sync", lambda e, m=m, dst=dst: e.dma_start(
                dst[m * 128:(m + 1) * 128, :], dtl_sb[m % 2][:, 0:TQ]),
                waits=((s_v, vm2),), incs=((s_res[m % 2], 16),)))
        R.barrier()
        if PHASES == 14:
            return _early_exit([esA2, esA])
    esA2.close()
    esA.close()

    esB = ExitStack()
    bsb = lambda n, shp, dt: esB.enter_context(nc.sbuf_tensor(n, list(shp), dt))
    O_sb = bsb("O", [128, NH * TQ], F32)
    bvbc_sb = bsb("bvbc", [128, H], BF16)
    K_sb = bsb("K", [128, NH * MEGA], BF16)
    V_sb = bsb("V", [128, KSUB * H], BF16)
    xnm_sb = bsb("xnm", [128, HC * MEGA], BF16)
    wvb_sb = bsb("wvb", [128, (HC // 2) * VFW], BF16)
    wkb_sb = [bsb(f"wkb{i}", [128, H], BF16) for i in range(2)]
    PT_sb = [bsb(f"PT{i}", [128, QGW], BF16) for i in range(4)]
    ksq_sb = bsb("ksq", [128, MEGA], BF16)
    xstB = [bsb(f"xstB{i}", [128, MEGA], F32) for i in range(2)]
    xsqB = [bsb(f"xsqB{i}", [128, MEGA], F32) for i in range(2)]
    srowB_sb = bsb("srowB", [1, 3 * MEGA], F32)

    if PHASES < 2:
        R.op("sync", lambda e: e.nop(),
             waits=((s_res0, CN[s_res0]), (s_res1, CN[s_res1])))
        R.build()
        esB.close()
        es.close()
        nc.finalize()
        nc._final_counts = {getattr(s, 'name', str(s)): v for s, v in CN.items()}
        return nc, c
    resw = ((s_res0, CN[s_res0]), (s_res1, CN[s_res1]))
    R.op("sync", lambda e: e.nop(), waits=resw)
    R.op("gpsimd", lambda e: e.nop(), waits=resw)
    R.op("gpsimd", lambda e: e.dma_start(bvbc_sb[:, :], bv_bc), incs=((s_dmg, 16),))
    bvtok = (s_dmg, CN[s_dmg])
    T(lambda e: e.nop(), waits=resw)
    SC(lambda e: e.nop(), waits=resw)
    V(lambda e: e.memset(O_sb[:, :], 0.0), waits=resw)
    V(lambda e: e.memset(Lacc_sb[:, :], 0.0))
    R.barrier()

    for M in range(NMEGA):
        emit_ln(M * MEGA, MEGA, xstB, xsqB,
                lambda cc: xnm_sb[:, cc * MEGA:(cc + 1) * MEGA],
                None,
                sumbanks=[(PS_X0, PS_X1)],
                bcbanks=[(PS_X0, PS_X1)], srow_sb=srowB_sb, SEG=MEGA)
        R.barrier()
        kevac = {}
        for h in range(NH):
            w = [(s_v, kevac[h - 2])] if h >= 2 else []
            ktok = swc.load(lambda e, h=h: e.dma_start(wkb_sb[h % 2][:, 0:H], wk_cs[h]), waits=w)
            bank = SBANK[h % 2]
            waits = [ktok]
            if h >= 2:
                waits.append((s_v, kevac[h - 2]))
            for k in range(HC):
                mk = T(lambda e, h=h, k=k, bank=bank: e.matmul(
                    bank[:, 0:MEGA], wkb_sb[h % 2][:, k * 128:(k + 1) * 128],
                    xnm_sb[:, k * MEGA:(k + 1) * MEGA],
                    start=(k == 0), stop=(k == HC - 1)),
                  waits=waits if k == 0 else (), mark=(k == HC - 1))
            V(lambda e, h=h, bank=bank: e.tensor_scalar_add(
                K_sb[:, h * MEGA:(h + 1) * MEGA], bank[:, 0:MEGA], bq_sb[:, HC + h:HC + h + 1]),
              waits=((s_mm, mk),))
            kevac[h] = V(lambda e, h=h: e.tensor_mul(
                ksq_sb[:, 0:MEGA], K_sb[:, h * MEGA:(h + 1) * MEGA],
                K_sb[:, h * MEGA:(h + 1) * MEGA]))
            mk = T(lambda e, h=h: e.matmul(
                PS_O0[0:NH, 0:MEGA], indb_sb[:, h * NH:h * NH + NH], ksq_sb[:, 0:MEGA],
                start=(h == 0), stop=(h == NH - 1)),
              waits=((s_v, kevac[h]),), mark=(h == NH - 1))
        if DBG:
            R.op("sync", lambda e, M=M: e.dma_start(
                dbgK[M * 128:(M + 1) * 128, :], K_sb[:, :]),
                waits=((s_v, CN[s_v]),), incs=((s_dm, 16),))
        V(lambda e: e.reduce_max(stc(C_T0), PS_O0[0:NH, 0:MEGA], AX.X), waits=((s_mm, mk),))
        vm = CN[s_v]
        SC(lambda e: e.activation(stc(C_T0), stc(C_T0), AF.Ln), waits=((s_v, vm),))
        scm = SC(lambda e: e.activation(stc(C_KMAX), stc(C_T0), AF.Exp, scale=0.5))
        V(lambda e: e.tensor_mul(stc(C_T1), stc(C_QMAX), stc(C_KMAX)), waits=((s_sc, scm),))
        V(lambda e: e.tensor_scalar_mul(stc(C_T1), stc(C_T1), SCALE))
        if M == 0:
            V(lambda e: e.tensor_copy(stc(C_SB), stc(C_T1)))
        else:
            V(lambda e: e.tensor_max(stc(C_SBN), stc(C_SB), stc(C_T1)))
            vm = V(lambda e: e.tensor_sub(stc(C_T2), stc(C_SB), stc(C_SBN)))
            scm = SC(lambda e: e.activation(stc(C_FAC), stc(C_T2), AF.Exp),
                     waits=((s_v, vm),))
            V(lambda e: e.tensor_copy(stc(C_SB), stc(C_SBN)), waits=((s_sc, scm),))
        if M > 0:
            vm = V(lambda e: e.transpose(rowt_sb[:, :], stat_sb[0:32, C_FAC:C_FAC + 32]))
            tm = T(lambda e: e.matmul(PS_X1[:, 0:NH], ones1_sb[:, :],
                                      rowt_sb[0:1, 0:NH], start=True, stop=True),
                   waits=((s_v, vm),), mark=True)
            V(lambda e: e.tensor_copy(facbc_sb[:, :], PS_X1[:, 0:NH]), waits=((s_mm, tm),))
            for g in range(NQG):
                V(lambda e, g=g: e.tensor_scalar_mul(
                    Lacc_sb[0:NH, g * QGW:(g + 1) * QGW],
                    Lacc_sb[0:NH, g * QGW:(g + 1) * QGW], stc(C_FAC)),
                  waits=((s_dm, CN[s_dm]),))
            for h in range(NH):
                for g in range(NQG):
                    V(lambda e, h=h, g=g: e.tensor_scalar_mul(
                        O_sb[:, h * TQ + g * QGW:h * TQ + (g + 1) * QGW],
                        O_sb[:, h * TQ + g * QGW:h * TQ + (g + 1) * QGW],
                        facbc_sb[:, h:h + 1]))
        V(lambda e: e.transpose(rowt_sb[:, :], stat_sb[0:32, C_SB:C_SB + 32]))
        V(lambda e: e.tensor_scalar(biasrow_sb[0:1, 96:96 + NH], rowt_sb[0:1, 0:NH],
                                    -1.0, CUSH, ALU.mult, ALU.add))
        for ks in range(KSUB):
            V(lambda e, ks=ks, M=M: e.tensor_scalar_add(
                biasrow_sb[0:1, ks * NH:(ks + 1) * NH],
                biasrow_sb[0:1, 96:96 + NH],
                flag_sb[0:1, M * KSUB + ks:M * KSUB + ks + 1]))
        vm = CN[s_v]
        tm = T(lambda e: e.matmul(PS_X0[:, 0:NH * KSUB], ones1_sb[:, :],
                                  biasrow_sb[0:1, 0:NH * KSUB], start=True, stop=True),
               waits=((s_v, vm),), mark=True)
        V(lambda e: e.tensor_copy(bias_sb[:, 0:NH * KSUB], PS_X0[:, 0:NH * KSUB]),
          waits=((s_mm, tm),))
        bias_ready = CN[s_v]
        R.barrier()
        vevac = {}
        vln = {}
        nhalf = HC // 2 if HC > 1 else 1
        for F4 in range(VFG):
            for sub in range(2 if HC > 1 else 1):
                idx = F4 * 2 + sub if HC > 1 else F4
                w = [(s_mm, vln[idx - 1])] if idx >= 1 else []
                if idx >= 1:
                    w.append((s_v, vevac[idx - 1]))
                vtok = swv.load(lambda e, F4=F4, c0=sub * nhalf: e.dma_start(
                    wvb_sb[:, 0:nhalf * VFW],
                    wv_rs[F4][:, c0 * VFW:(c0 + nhalf) * VFW]), waits=w)
                c0 = sub * nhalf
                for ks in range(KSUB):
                    bank = [PS_O0, PS_O1, PS_X0, PS_X1][ks]
                    for ci in range(nhalf):
                        k = c0 + ci
                        mk = T(lambda e, k=k, ci=ci, ks=ks, bank=bank: e.matmul(
                            bank[:, 0:VFW], xnm_sb[:, k * MEGA + ks * 128:k * MEGA + ks * 128 + 128],
                            wvb_sb[:, ci * VFW:(ci + 1) * VFW],
                            start=(k == 0), stop=(k == HC - 1)),
                          waits=(vtok,) if ci == 0 else (),
                          mark=(ci == nhalf - 1))
                    if sub == (1 if HC > 1 else 0):
                        V(lambda e, F4=F4, ks=ks, bank=bank: e.scalar_tensor_tensor(
                            V_sb[:, ks * H + F4 * VFW:ks * H + (F4 + 1) * VFW],
                            bank[:, 0:VFW], 1.0,
                            bvbc_sb[:, F4 * VFW:(F4 + 1) * VFW], ALU.mult, ALU.add),
                          waits=((s_mm, mk), bvtok))
                vevac[idx] = CN[s_v]
                vln[idx] = CN[s_mm]
        vready = CN[s_v]
        R.barrier()
        exp_marks = {}
        pv_marks = {}
        oevac = {}
        lmk = {}
        for h in range(NH):
            for ks in range(KSUB):
                kslot = M * KSUB + ks
                kpar = ks % 2
                j = kslot - OWN0
                mkq = {}
                for g in range(NQG):
                    bidx = kpar * 2 + g
                    sbank = [PS_S0, PS_S1, PS_X0, PS_X1][bidx]
                    waits = [(s_v, vready)]
                    if bidx in exp_marks:
                        waits.append((s_sc, exp_marks[bidx]))
                    mkq[g] = T(lambda e, h=h, g=g, ks=ks, sbank=sbank: e.matmul(
                        sbank[:, 0:QGW], K_sb[:, h * MEGA + ks * 128:h * MEGA + ks * 128 + 128],
                        Q_sb[:, h * TQ + g * QGW:h * TQ + (g + 1) * QGW],
                        start=True, stop=True),
                      waits=waits, mark=True)
                for g in range(NQG):
                    bidx = kpar * 2 + g
                    sbank = [PS_S0, PS_S1, PS_X0, PS_X1][bidx]
                    trim = None
                    if j >= 0:
                        for qq in range(QSUBG):
                            if j == g * QSUBG + qq:
                                trim = V(lambda e, qq=qq, sbank=sbank: e.tensor_tensor(
                                    sbank[:, qq * 128:(qq + 1) * 128],
                                    sbank[:, qq * 128:(qq + 1) * 128],
                                    tri_sb[:, :], ALU.add), waits=((s_mm, mkq[g]),))
                    ew = [(s_mm, mkq[g]), (s_v, max(bias_ready, trim or 0))]
                    if (kpar, g) in pv_marks:
                        ew.append((s_mm, pv_marks[(kpar, g)]))
                    em = SC(lambda e, ks=ks, h=h, sbank=sbank, bidx=bidx: e.activation(
                        PT_sb[bidx][:, 0:QGW], sbank[:, 0:QGW], AF.Exp,
                        bias=bias_sb[:, ks * NH + h:ks * NH + h + 1], scale=SCALE),
                        waits=ew)
                    exp_marks[bidx] = em
                    memm = None
                    if j >= 0:
                        for qq in range(QSUBG):
                            if j > g * QSUBG + qq:
                                memm = V(lambda e, qq=qq, bidx=bidx: e.memset(
                                    PT_sb[bidx][:, qq * 128:(qq + 1) * 128], 0.0),
                                    waits=((s_sc, em),))
                    exp_marks[(bidx, "mem")] = memm
                for g in range(NQG):
                    bidx = kpar * 2 + g
                    obank = [PS_O0, PS_O1][g]
                    pvw = [(s_sc, exp_marks[bidx])]
                    if exp_marks.get((bidx, "mem")):
                        pvw.append((s_v, exp_marks[(bidx, "mem")]))
                    if ks == 0 and (h - 1, g) in oevac:
                        pvw.append((s_v, oevac[(h - 1, g)]))
                    T(lambda e, ks=ks, h=h, bidx=bidx, obank=obank: e.matmul(
                        obank[:, 0:QGW], V_sb[:, ks * H + h * 128:ks * H + h * 128 + 128],
                        PT_sb[bidx][:, 0:QGW], start=(ks == 0), stop=(ks == KSUB - 1)),
                      waits=pvw)
                for g in range(NQG):
                    bidx = kpar * 2 + g
                    mk2 = T(lambda e, ks=ks, h=h, g=g, bidx=bidx: e.matmul(
                        LBANK[g][0:NH, 0:QGW], indb_sb[:, h * NH:h * NH + NH],
                        PT_sb[bidx][:, 0:QGW],
                        start=(h == 0 and ks == 0),
                        stop=(h == NH - 1 and ks == KSUB - 1)),
                      mark=True)
                    pv_marks[(kpar, g)] = mk2
                    lmk[g] = mk2
            for g in range(NQG):
                obank = [PS_O0, PS_O1][g]
                oevac[(h, g)] = V(lambda e, h=h, g=g, obank=obank: e.tensor_tensor(
                    O_sb[:, h * TQ + g * QGW:h * TQ + (g + 1) * QGW],
                    O_sb[:, h * TQ + g * QGW:h * TQ + (g + 1) * QGW],
                    obank[:, 0:QGW], ALU.add), waits=((s_mm, pv_marks[(KSUB - 1) % 2, g] if False else lmk[g]),))
        for g in range(NQG):
            V(lambda e, g=g: e.tensor_tensor(
                Lacc_sb[0:NH, g * QGW:(g + 1) * QGW],
                Lacc_sb[0:NH, g * QGW:(g + 1) * QGW],
                LBANK[g][0:NH, 0:QGW], ALU.add), waits=((s_mm, lmk[g]),))
        if DBG:
            R.op("sync", lambda e, M=M: e.dma_start(
                dbgL[M * 16:M * 16 + 16, :], Lacc_sb[:, :]),
                waits=((s_v, CN[s_v]),), incs=((s_dm, 16),))
        R.barrier()
    esB.close()
    if PHASES < 3:
        R.op("sync", lambda e: e.nop(),
             waits=((s_res0, CN[s_res0]), (s_res1, CN[s_res1]), (s_dm, CN[s_dm]), (s_dmg, CN[s_dmg])))
        R.build()
        es.close()
        nc.finalize()
        nc._final_counts = {getattr(s, 'name', str(s)): v for s, v in CN.items()}
        return nc, c

    esC = ExitStack()
    csb = lambda n, shp, dt: esC.enter_context(nc.sbuf_tensor(n, list(shp), dt))
    Obf_sb = csb("Obf", [128, NH * TQ], BF16)
    wob_sb = [csb(f"wob{i}", [128, H], BF16) for i in range(2)]
    ytl_sb = [csb(f"ytl{i}", [128, QGW], F32) for i in range(2)]
    rtl_sb = [csb(f"rtl{i}", [128, TQ], F32) for i in range(2)]

    for g in range(NQG):
        V(lambda e, g=g: e.reciprocal(rL_sb[0:NH, g * QGW:(g + 1) * QGW],
                                      Lacc_sb[0:NH, g * QGW:(g + 1) * QGW]))
    vm = CN[s_v]
    srl2 = Dstream("srl2", "gpsimd")
    bcmk = {}
    for h in range(NH):
        for g in range(NQG):
            i = h * NQG + g
            slot = i % 2
            bank = [PS_X0, PS_X1][i % 2]
            w = [(s_v, vm)]
            if i >= 2:
                w.append((s_mm, bcmk[i - 2]))
            ltok = srl2.load(lambda e, h=h, g=g, slot=slot: e.dma_start(
                rl0_sb[0:1, slot * QGW:(slot + 1) * QGW],
                rL_sb[h:h + 1, g * QGW:(g + 1) * QGW]), waits=w)
            w2 = [ltok]
            if i >= 2:
                w2.append((s_v, CN[s_v]))
            tm = T(lambda e, slot=slot, bank=bank: e.matmul(
                bank[:, 0:QGW], ones1_sb[:, :],
                rl0_sb[0:1, slot * QGW:(slot + 1) * QGW],
                start=True, stop=True), waits=w2, mark=True)
            bcmk[i] = tm
            V(lambda e, h=h, g=g, bank=bank: e.tensor_tensor(
                Obf_sb[:, h * TQ + g * QGW:h * TQ + (g + 1) * QGW],
                O_sb[:, h * TQ + g * QGW:h * TQ + (g + 1) * QGW],
                bank[:, 0:QGW], ALU.mult), waits=((s_mm, tm),))
    R.barrier()
    yev = {}
    otmk = {}
    for m in range(HC):
        w = [(s_v, yev[(m - 2, NQG - 1)])] if m >= 2 else []
        wtok = swc.load(lambda e, m=m: e.dma_start(wob_sb[m % 2][:, 0:H], wo_cs[m]), waits=w)
        ow = [otmk[m - 2]] if m >= 2 else []
        rtok = stm.load(lambda e, m=m: e.dma_start(rtl_sb[m % 2][:, 0:TQ],
                                                   resid_hbm[m * 128:(m + 1) * 128, :]),
                        waits=[(s_res0, CN[s_res0]), (s_res1, CN[s_res1])] + w + ow)
        waits = [wtok]
        if m >= 1:
            waits.append((s_v, yev[(m - 1, NQG - 1)]))
        mks = {}
        for k in range(HC):
            for gq in range(NQG):
                bank = SBANK[gq]
                mks[gq] = T(lambda e, m=m, gq=gq, k=k, bank=bank: e.matmul(
                    bank[:, 0:QGW], wob_sb[m % 2][:, k * 128:(k + 1) * 128],
                    Obf_sb[:, k * TQ + gq * QGW:k * TQ + (gq + 1) * QGW],
                    start=(k == 0), stop=(k == HC - 1)),
                  waits=waits if (k == 0 and gq == 0) else (), mark=(k == HC - 1))
        for gq in range(NQG):
            bank = SBANK[gq]
            V(lambda e, m=m, gq=gq, bank=bank: e.tensor_scalar_add(
                ytl_sb[gq % 2][:, 0:QGW], bank[:, 0:QGW], bo_sb[:, m:m + 1]),
              waits=((s_mm, mks[gq]), rtok))
            yev[(m, gq)] = V(lambda e, m=m, gq=gq: e.tensor_tensor(
                rtl_sb[m % 2][:, gq * QGW:(gq + 1) * QGW],
                rtl_sb[m % 2][:, gq * QGW:(gq + 1) * QGW],
                ytl_sb[gq % 2][:, 0:QGW], ALU.add))
        vm2 = CN[s_v]
        otmk[m] = (s_out[m % 2], R.op("sync", lambda e, m=m: e.dma_start(
            outT[m * 128:(m + 1) * 128, :], rtl_sb[m % 2][:, 0:TQ]),
            waits=((s_v, vm2),), incs=((s_out[m % 2], 16),)))
    R.op("sync", lambda e: e.nop(),
         waits=((s_out0, CN[s_out0]), (s_out1, CN[s_out1]), (s_dm, CN[s_dm])))

    R.build()
    esC.close()
    es.close()
    nc.finalize()
    nc._final_counts = {getattr(s, 'name', str(s)): v for s, v in CN.items()}
    return nc, c


def stage_inputs(cfg, x, Wqkv, bqkv, Wout, bout, gw, uw, dw):
    c = derive(cfg)
    H, NH, TQ, S, FFI = c["H"], c["NH"], c["TQ"], c["S"], c["FFI"]
    HC, VFW, VFG, FFB = c["HC"], c["VFW"], c["VFG"], c["FFB"]
    NKSLOT, OWN0 = c["NKSLOT"], c["OWN0"]
    f32 = np.float32
    ac = np.ascontiguousarray

    def colslab(W, nb):
        K = W.shape[1]
        return ac(W.reshape(nb, 128, K // 128, 128).transpose(0, 3, 2, 1)
                  .reshape(nb, 128, K)).astype(f32)

    Wq, Wk, Wv = Wqkv[:H], Wqkv[H:2 * H], Wqkv[2 * H:]
    shared = {
        "wq_cs": colslab(Wq, NH),
        "wk_cs": colslab(Wk, NH),
        "wv_rs": ac(Wv.reshape(VFG, VFW, HC, 128).transpose(0, 3, 2, 1)
                    .reshape(VFG, 128, HC * VFW)).astype(f32),
        "wo_cs": colslab(Wout, HC),
        "wg_rows": ac(gw.reshape(FFB, 128, H)).astype(f32),
        "wu_rows": ac(uw.reshape(FFB, 128, H)).astype(f32),
        "wd_rows": ac(dw.reshape(HC, 128, FFI)).astype(f32),
        "bqkv_t": ac(bqkv.reshape(3 * HC, 128).T).astype(f32),
        "bout_t": ac(bout.reshape(HC, 128).T).astype(f32),
        "bv_bc": ac(np.broadcast_to(bqkv[2 * H:3 * H], (128, H))).astype(np.float32).astype(ml_dtypes.bfloat16),
        "tri_in": np.where(np.arange(128)[:, None] <= np.arange(128)[None, :],
                        np.float32(0.0), np.float32(NEG)),
    }
    ind = np.zeros((128, NH * NH), dtype=f32)
    for h in range(NH):
        ind[:, h * NH + h] = 1.0
    shared["ind_b"] = ind.astype(ml_dtypes.bfloat16)
    shared["id_in"] = np.eye(128, dtype=f32)

    NCHUNK = S // TQ
    in_maps = []
    for core in range(2 * NCHUNK):
        b, j = core // NCHUNK, core % NCHUNK
        xb = np.asarray(x[b], dtype=f32)
        plen = j * TQ
        xsT = np.zeros((H, S), dtype=f32)
        if plen:
            xsT[:, :plen] = xb[:plen].T
        xsT[:, S - TQ:] = xb[j * TQ:(j + 1) * TQ].T
        flag = np.zeros((128, NKSLOT), dtype=f32)
        flag[:, plen // 128:OWN0] = NEG
        m = dict(shared)
        m["xs"] = ac(xsT)
        m["flag_in"] = flag
        in_maps.append(m)
    return in_maps


_NC_CACHE = {}


def kernel(L0_attn_x, L0_attn_W_qkv, L0_attn_b_qkv, L0_attn_W_out, L0_attn_b_out,
           L0_ffn_gate_w, L0_ffn_up_w, L0_ffn_down_w):
    cfg = FULL_CFG
    c = derive(cfg)
    x = np.asarray(L0_attn_x, dtype=np.float32)
    in_maps = stage_inputs(cfg, x,
                           np.asarray(L0_attn_W_qkv, dtype=np.float32),
                           np.asarray(L0_attn_b_qkv, dtype=np.float32),
                           np.asarray(L0_attn_W_out, dtype=np.float32),
                           np.asarray(L0_attn_b_out, dtype=np.float32),
                           np.asarray(L0_ffn_gate_w, dtype=np.float32),
                           np.asarray(L0_ffn_up_w, dtype=np.float32),
                           np.asarray(L0_ffn_down_w, dtype=np.float32))
    key = tuple(sorted(cfg.items()))
    if key not in _NC_CACHE:
        _NC_CACHE[key] = build_nc(cfg)
    nc, _ = _NC_CACHE[key]
    res = run_bass_kernel_spmd(nc, in_maps, core_ids=list(range(8)))
    TQ = c["TQ"]
    out = np.empty((2, c["S"], c["H"]), dtype=np.float32)
    for core in range(8):
        b, j = core // 4, core % 4
        out[b, j * TQ:(j + 1) * TQ, :] = np.asarray(res.results[core]["outT"]).T
    return out


# version 08
# speedup vs baseline: 1.3794x; 1.1033x over previous
import numpy as np
import ml_dtypes
from contextlib import ExitStack

import concourse.bass as bass
import concourse.mybir as mybir
from concourse.bass_utils import run_bass_kernel_spmd

F32 = mybir.dt.float32
BF16 = mybir.dt.bfloat16
FP8 = mybir.dt.float8e4
AF = mybir.ActivationFunctionType
ALU = mybir.AluOpType
AX = mybir.AxisListType

CUSH = 70.0
EPS = 1e-5
NEG = -1e30

FULL_CFG = dict(H=2048, NH=16, TQ=1024, S=4096, FFI=8192)


def derive(cfg):
    c = dict(cfg)
    c["HC"] = c["H"] // 128
    c["SCALE"] = float(1.0 / np.sqrt(128.0))
    c["QGW"] = min(512, c["TQ"])
    c["NQG"] = c["TQ"] // c["QGW"]
    c["QSUBG"] = c["QGW"] // 128
    c["MEGA"] = min(512, c["S"])
    c["NMEGA"] = c["S"] // c["MEGA"]
    c["KSUB"] = c["MEGA"] // 128
    c["NKSLOT"] = c["S"] // 128
    c["NQS"] = c["TQ"] // 128
    c["OWN0"] = c["NKSLOT"] - c["NQS"]
    c["VFW"] = min(512, c["H"])
    c["VFG"] = c["H"] // c["VFW"]
    c["FFB"] = c["FFI"] // 128
    c["NCH"] = max(2, c["FFI"] // 2048)
    c["FFHALF"] = c["FFI"] // c["NCH"]
    c["DQ"] = min(2048, c["FFHALF"])
    c["NDQ"] = c["FFHALF"] // c["DQ"]
    return c


class Rec:
    ENGS = ("sync", "gpsimd", "tensor", "vector", "scalar")

    def __init__(self, nc):
        self.nc = nc
        self.stages = [[]]
        self.CN = {}
        self.es = ExitStack()

    def sem(self, name):
        s = self.es.enter_context(self.nc.semaphore(name))
        self.CN[s] = 0
        return s

    def barrier(self):
        if self.stages[-1]:
            self.stages.append([])

    def op(self, eng, fn, waits=(), incs=()):
        rw = tuple((s, int(v)) for (s, v) in waits)
        ri = []
        out = None
        for (s, k) in incs:
            self.CN[s] += k
            ri.append((s, k))
            if out is None:
                out = self.CN[s]
        self.stages[-1].append((eng, fn, rw, tuple(ri)))
        return out

    def build(self):
        for stage in self.stages:
            if not stage:
                continue
            with self.nc.Block() as blk:
                for engname in self.ENGS:
                    ops = [o for o in stage if o[0] == engname]
                    if not ops:
                        continue
                    def body(eng, ops=ops):
                        for (_, fn, waits, incs) in ops:
                            for (s, v) in waits:
                                eng.wait_ge(s, v)
                            inst = fn(eng)
                            for (s, k) in incs:
                                inst.then_inc(s, k)
                    getattr(blk, engname)(body)


def build_nc(cfg):
    c = derive(cfg)
    H, NH, TQ, S, FFI = c["H"], c["NH"], c["TQ"], c["S"], c["FFI"]
    HC, SCALE = c["HC"], c["SCALE"]
    QGW, NQG, QSUBG = c["QGW"], c["NQG"], c["QSUBG"]
    MEGA, NMEGA, KSUB = c["MEGA"], c["NMEGA"], c["KSUB"]
    NKSLOT, OWN0 = c["NKSLOT"], c["OWN0"]
    VFW, VFG = c["VFW"], c["VFG"]
    FFB, FFHALF, DQ, NDQ = c["FFB"], c["FFHALF"], c["DQ"], c["NDQ"]
    NCH = c["NCH"]
    HINV = float(1.0 / H)
    NHB = FFHALF // 128
    SEGW = max(TQ, MEGA)

    PHASES = int(cfg.get("phases", 3))
    FFNLITE = int(cfg.get("ffnlite", 3))
    nc = bass.Bass("TRN2", target_bir_lowering=False)
    R = Rec(nc)

    def din(name, shape, dt=F32):
        return nc.dram_tensor(name, list(shape), dt, kind="ExternalInput").ap()

    xs = din("xs", (H, S))
    wq_cs = din("wq_cs", (NH, 128, H))
    wk_cs = din("wk_cs", (NH, 128, H))
    wv_rs = din("wv_rs", (VFG, 128, HC * VFW))
    wo_cs = din("wo_cs", (HC, 128, H))
    wg_rows = din("wg_rows", (FFB, 128, H))
    wu_rows = din("wu_rows", (FFB, 128, H))
    wd_rows = din("wd_rows", (HC, 128, FFI))
    bqkv_t = din("bqkv_t", (128, 3 * HC))
    bout_t = din("bout_t", (128, HC))
    bv_bc = din("bv_bc", (128, H), BF16)
    tri_c = din("tri_in", (128, 128))
    flag_c = din("flag_in", (128, NKSLOT))
    ind_b = din("ind_b", (128, NH * NH), BF16)
    id_in = din("id_in", (128, 128))

    dacc_hbm = nc.dram_tensor("dacc_hbm", [H, TQ], F32, kind="Internal").ap()
    resid_hbm = nc.dram_tensor("resid_hbm", [H, TQ], F32, kind="Internal").ap()
    outT = nc.dram_tensor("outT", [H, TQ], F32, kind="ExternalOutput").ap()
    DBG = bool(cfg.get("dbg"))
    if DBG:
        dbgK = nc.dram_tensor("dbgK", [NMEGA * 128, NH * MEGA], BF16, kind="ExternalOutput").ap()
        dbgL = nc.dram_tensor("dbgL", [NMEGA * 16, NQG * QGW], F32, kind="ExternalOutput").ap()
        dbgPT = nc.dram_tensor("dbgPT", [NMEGA * 128, NH * KSUB * QGW], BF16, kind="ExternalOutput").ap()

    es = R.es
    sbuf = lambda n, shp, dt: es.enter_context(nc.sbuf_tensor(n, list(shp), dt))
    mkps = lambda n: es.enter_context(nc.psum_tensor(n, [128, 512], F32))

    s_dm = R.sem("s_dm")
    s_dmg = R.sem("s_dmg")
    s_res0 = R.sem("s_res0")
    s_res1 = R.sem("s_res1")
    s_res = [s_res0, s_res1]
    s_out0 = R.sem("s_out0")
    s_out1 = R.sem("s_out1")
    s_out = [s_out0, s_out1]
    s_mm = R.sem("s_mm")
    s_v = R.sem("s_v")
    s_sc = R.sem("s_sc")
    s_rl = R.sem("s_rl")

    CN = R.CN

    class Dstream:
        def __init__(self, name, queue, npar=2):
            self.sems = [R.sem(f"{name}{i}") for i in range(npar)]
            self.queue = queue
            self.n = 0
        def load(self, fn, waits=()):
            s = self.sems[self.n % len(self.sems)]
            self.n += 1
            cnt = R.op(self.queue, fn, waits=waits, incs=((s, 16),))
            return (s, cnt)

    stx = Dstream("stx", "sync")
    strow = Dstream("strow", "sync")
    stm = Dstream("stm", "sync")
    swc = Dstream("swc", "gpsimd")
    swv = Dstream("swv", "gpsimd")

    def V(fn, waits=()):
        return R.op("vector", fn, waits=tuple(waits) + ((s_v, CN[s_v]),), incs=((s_v, 1),))

    def SC(fn, waits=()):
        return R.op("scalar", fn, waits=tuple(waits) + ((s_sc, CN[s_sc]),), incs=((s_sc, 1),))

    def T(fn, waits=(), mark=False):
        return R.op("tensor", fn, waits=waits, incs=((s_mm, 1),) if mark else ())

    Q_sb = sbuf("Q", [128, NH * TQ], BF16)
    bq_sb = sbuf("bq", [128, 3 * HC], F32)
    bo_sb = sbuf("bo", [128, HC], F32)
    tri_sb = sbuf("tri", [128, 128], F32)
    flag_sb = sbuf("flag", [128, NKSLOT], F32)
    indb_sb = sbuf("indb", [128, NH * NH], BF16)
    ones1_sb = sbuf("ones1", [1, 128], F32)
    onesc_sb = sbuf("onesc", [128, 1], F32)
    stat_sb = sbuf("stat", [32, 64], F32)
    rowt_sb = sbuf("rowt", [32, 32], F32)
    biasrow_sb = sbuf("biasrow", [1, 4 * 32], F32)
    bias_sb = sbuf("biasx", [128, KSUB * NH], F32)
    facbc_sb = sbuf("facbc", [128, NH], F32)
    rl0_sb = sbuf("rl0", [1, 2 * 512], F32)
    rL_sb = sbuf("rL", [16, NQG * QGW], F32)
    Lacc_sb = sbuf("Lacc", [16, NQG * QGW], F32)
    id128_sb = sbuf("id128", [128, 128], F32)
    tqT_sb = sbuf("tqT", [128, 512], F32)
    tq8_sb = sbuf("tq8", [128, 512], FP8)
    tqb_sb = sbuf("tqb", [128, 512], BF16)
    bmt_sb = sbuf("bmt", [128, 16], F32)
    rsbt_sb = sbuf("rsbt", [128, 16], F32)
    s2bt_sb = sbuf("s2bt", [128, 16], F32)

    C_QMAX, C_KMAX, C_SB, C_SBN, C_FAC, C_T0, C_T1, C_T2 = range(8)
    stc = lambda col: stat_sb[0:NH, col:col + 1]

    PS_S0 = mkps("ps_s0"); PS_S1 = mkps("ps_s1")
    PS_O0 = mkps("ps_o0"); PS_O1 = mkps("ps_o1")
    PS_L0 = mkps("ps_l0"); PS_L1 = mkps("ps_l1")
    PS_X0 = mkps("ps_x0"); PS_X1 = mkps("ps_x1")
    SBANK = [PS_S0, PS_S1]
    LBANK = [PS_L0, PS_L1]
    XBANK2 = [PS_X0, PS_X1]

    for apv, dst in ((bqkv_t, bq_sb), (bout_t, bo_sb), (tri_c, tri_sb),
                     (flag_c, flag_sb), (id_in, id128_sb)):
        R.op("sync", lambda e, apv=apv, dst=dst: e.dma_start(dst[:, :], apv),
             incs=((s_dm, 16),))
    R.op("gpsimd", lambda e: e.dma_start(indb_sb[:, :], ind_b), incs=((s_dmg, 16),))
    V(lambda e: e.memset(ones1_sb[:, :], 1.0))
    V(lambda e: e.memset(onesc_sb[:, :], 1.0))
    V(lambda e: e.memset(stat_sb[:, :], 0.0))
    V(lambda e: e.memset(rowt_sb[:, :], 0.0))
    V(lambda e: e.memset(biasrow_sb[:, :], 0.0))
    cdm = CN[s_dm]
    cdw = CN[s_dmg]
    R.barrier()
    T(lambda e: e.nop(), waits=((s_dm, cdm), (s_dmg, cdw)))
    V(lambda e: e.nop(), waits=((s_dm, cdm), (s_dmg, cdw)))
    SC(lambda e: e.nop(), waits=((s_dm, cdm), (s_dmg, cdw)))

    def emit_quantT(src_f32_ap, dst_bf16_ap, width=512):
        nj = width // 32
        V(lambda e: e.transpose(tqT_sb[:, 0:width], src_f32_ap))
        V(lambda e: e.tensor_reduce(
            bmt_sb[:, 0:nj], tqT_sb[:, 0:width].rearrange("p (j b) -> p j b", j=nj),
            AX.X, ALU.max, apply_absolute_value=True))
        V(lambda e: e.tensor_scalar(s2bt_sb[:, 0:nj], bmt_sb[:, 0:nj],
                                    1.0 / 448.0, 1e-30, ALU.mult, ALU.max))
        V(lambda e: e.reciprocal(rsbt_sb[:, 0:nj], s2bt_sb[:, 0:nj]))
        V(lambda e: e.tensor_scalar_mul(rsbt_sb[:, 0:nj], rsbt_sb[:, 0:nj], 0.5))
        V(lambda e: e.tensor_scalar_mul(s2bt_sb[:, 0:nj], s2bt_sb[:, 0:nj], 2.0))
        V(lambda e: e.tensor_tensor(
            tq8_sb[:, 0:width], tqT_sb[:, 0:width],
            rsbt_sb[:, 0:nj].broadcast_to([128, nj, 32]), ALU.mult))
        V(lambda e: e.tensor_tensor(
            tqb_sb[:, 0:width], tq8_sb[:, 0:width],
            s2bt_sb[:, 0:nj].broadcast_to([128, nj, 32]), ALU.mult))
        V(lambda e: e.transpose(dst_bf16_ap, tqb_sb[:, 0:width]))

    def emit_wquant(K, rowbuf, dqbuf, extra_waits=()):
        nkb = K // 32
        V(lambda e: e.tensor_reduce(
            bmw_sb[:, 0:nkb], rowbuf[:, 0:K].rearrange("p (b k) -> p b k", k=32),
            AX.X, ALU.max, apply_absolute_value=True), waits=extra_waits)
        V(lambda e: e.tensor_scalar(s2w_sb[:, 0:nkb], bmw_sb[:, 0:nkb],
                                    1.0 / 448.0, 1e-30, ALU.mult, ALU.max))
        V(lambda e: e.reciprocal(rsw_sb[:, 0:nkb], s2w_sb[:, 0:nkb]))
        V(lambda e: e.tensor_scalar_mul(s2w_sb[:, 0:nkb], s2w_sb[:, 0:nkb], 2.0))
        V(lambda e: e.scalar_tensor_tensor(
            q8w_sb[:, 0:K], rowbuf[:, 0:K], 0.5,
            rsw_sb[:, 0:nkb].broadcast_to([128, nkb, 32]), ALU.mult, ALU.mult))
        vm = V(lambda e: e.tensor_tensor(
            dqr_sb[:, 0:K], q8w_sb[:, 0:K],
            s2w_sb[:, 0:nkb].broadcast_to([128, nkb, 32]), ALU.mult))
        evm = {}
        ngrp4 = (K + 511) // 512
        for grp in range(ngrp4):
            bank = XBANK2[grp % 2]
            nin = min(4, K // 128 - grp * 4)
            w = [(s_v, vm)] if grp < 2 else [(s_v, evm[grp - 2])]
            for i4 in range(nin):
                cc2 = grp * 4 + i4
                tm = T(lambda e, cc2=cc2, i4=i4, bank=bank: e.transpose(
                    bank[:, i4 * 128:(i4 + 1) * 128],
                    dqr_sb[:, cc2 * 128:(cc2 + 1) * 128], id128_sb[:, :]),
                    waits=w if i4 == 0 else (), mark=(i4 == nin - 1))
                w = ()
            evm[grp] = V(lambda e, grp=grp, nin=nin, bank=bank: e.tensor_copy(
                dqbuf[:, grp * 512:grp * 512 + nin * 128], bank[:, 0:nin * 128]),
                waits=((s_mm, tm),))
        return evm

    def emit_ln(tok0, ntok, xst, xsq, xn_write, xnf_consumer,
                sumbanks, bcbanks, srow_sb, SEG):
        ngrp = max(1, ntok // 512)
        gw = ntok // ngrp
        sq_marks = {}
        mm_marks = {}
        dtok = {}
        for cc in range(HC):
            w = [(s_mm, mm_marks[cc - 2])] if cc >= 2 else []
            dtok[cc] = stx.load(lambda e, cc=cc: e.dma_start(
                xst[cc % 2][:, 0:ntok], xs[cc * 128:(cc + 1) * 128, tok0:tok0 + ntok]),
               waits=w)
            sq_marks[cc] = SC(lambda e, cc=cc: e.activation(
                xsq[cc % 2][:, 0:ntok], xst[cc % 2][:, 0:ntok], AF.Square),
                waits=[dtok[cc]] + ([(s_mm, mm_marks[cc - 2])] if cc >= 2 else []))
            for gq in range(ngrp):
                T(lambda e, cc=cc, gq=gq: e.matmul(
                    sumbanks[gq][0][0:1, 0:gw], onesc_sb[:, :],
                    xst[cc % 2][:, gq * gw:(gq + 1) * gw],
                    start=(cc == 0), stop=(cc == HC - 1)),
                  waits=(dtok[cc],))
            for gq in range(ngrp):
                mk = T(lambda e, cc=cc, gq=gq: e.matmul(
                    sumbanks[gq][1][0:1, 0:gw], onesc_sb[:, :],
                    xsq[cc % 2][:, gq * gw:(gq + 1) * gw],
                    start=(cc == 0), stop=(cc == HC - 1)),
                  waits=((s_sc, sq_marks[cc]),), mark=(gq == ngrp - 1))
            mm_marks[cc] = CN[s_mm]
        R.barrier()
        smw = [(s_mm, CN[s_mm])]
        for gq in range(ngrp):
            sgq = slice(gq * gw, (gq + 1) * gw)
            V(lambda e, sgq=sgq, gq=gq: e.tensor_scalar_mul(
                srow_sb[0:1, sgq], sumbanks[gq][0][0:1, 0:gw], HINV), waits=smw)
            smw = ()
            V(lambda e, sgq=sgq, gq=gq: e.tensor_scalar_mul(
                srow_sb[0:1, 2 * SEG + gq * gw:2 * SEG + (gq + 1) * gw],
                sumbanks[gq][1][0:1, 0:gw], HINV))
        V(lambda e: e.tensor_mul(srow_sb[0:1, SEG:SEG + ntok],
                                 srow_sb[0:1, 0:ntok], srow_sb[0:1, 0:ntok]))
        V(lambda e: e.tensor_sub(srow_sb[0:1, 2 * SEG:2 * SEG + ntok],
                                 srow_sb[0:1, 2 * SEG:2 * SEG + ntok], srow_sb[0:1, SEG:SEG + ntok]))
        vm = V(lambda e: e.tensor_scalar_add(srow_sb[0:1, 2 * SEG:2 * SEG + ntok], srow_sb[0:1, 2 * SEG:2 * SEG + ntok], EPS))
        SC(lambda e: e.activation(srow_sb[0:1, 2 * SEG:2 * SEG + ntok], srow_sb[0:1, 2 * SEG:2 * SEG + ntok], AF.Ln),
           waits=((s_v, vm),))
        scm = SC(lambda e: e.activation(srow_sb[0:1, 2 * SEG:2 * SEG + ntok], srow_sb[0:1, 2 * SEG:2 * SEG + ntok],
                                        AF.Exp, scale=-0.5))
        vm = V(lambda e: e.scalar_tensor_tensor(
            srow_sb[0:1, 0:ntok], srow_sb[0:1, 0:ntok], -1.0, srow_sb[0:1, 2 * SEG:2 * SEG + ntok],
            ALU.mult, ALU.mult), waits=((s_sc, scm),))
        for gq in range(ngrp):
            br, bs = bcbanks[gq]
            T(lambda e, gq=gq, br=br: e.matmul(
                br[:, 0:gw], ones1_sb[:, :], srow_sb[0:1, 2 * SEG + gq * gw:2 * SEG + (gq + 1) * gw],
                start=True, stop=True), waits=((s_v, vm),))
            T(lambda e, gq=gq, bs=bs: e.matmul(
                bs[:, 0:gw], ones1_sb[:, :], srow_sb[0:1, gq * gw:(gq + 1) * gw],
                start=True, stop=True), mark=(gq == ngrp - 1))
        bcm = CN[s_mm]
        v_marks = {}
        for cc in range(HC):
            w = [(s_v, v_marks[cc - 2])] if cc >= 2 else []
            dtok[cc] = stx.load(lambda e, cc=cc: e.dma_start(
                xst[cc % 2][:, 0:ntok], xs[cc * 128:(cc + 1) * 128, tok0:tok0 + ntok]),
               waits=w)
            first = [dtok[cc], (s_mm, bcm)]
            for gq in range(ngrp):
                br, bs = bcbanks[gq]
                sgq = slice(gq * gw, (gq + 1) * gw)
                V(lambda e, cc=cc, sgq=sgq, br=br: e.tensor_tensor(
                    xst[cc % 2][:, sgq], xst[cc % 2][:, sgq], br[:, 0:gw], ALU.mult),
                  waits=first)
                first = ()
                V(lambda e, cc=cc, sgq=sgq, bs=bs: e.tensor_tensor(
                    xst[cc % 2][:, sgq], xst[cc % 2][:, sgq], bs[:, 0:gw], ALU.add))
            wdst = xn_write(cc)
            if wdst is not None:
                V(lambda e, cc=cc, wdst=wdst: e.tensor_copy(wdst, xst[cc % 2][:, 0:ntok]))
            if xnf_consumer is not None:
                xnf_consumer(cc, xst[cc % 2])
            v_marks[cc] = CN[s_v]
        return CN[s_v]


    def _early_exit(stacks):
        R.op("sync", lambda e: e.nop(),
             waits=((s_res0, CN[s_res0]), (s_res1, CN[s_res1]),
                    (s_dm, CN[s_dm]), (s_dmg, CN[s_dmg])))
        R.build()
        for st in stacks:
            st.close()
        es.close()
        nc.finalize()
        nc._final_counts = {getattr(s, 'name', str(s)): v for s, v in CN.items()}
        return nc, c

    esA = ExitStack()
    asb = lambda n, shp, dt: esA.enter_context(nc.sbuf_tensor(n, list(shp), dt))
    xdq_sb = asb("xdq", [128, HC * TQ], BF16)
    xstA = [asb(f"xstA{i}", [128, TQ], F32) for i in range(2)]
    srowA_sb = asb("srowA", [1, 3 * TQ], F32)

    esA1 = ExitStack()
    xn_sb = esA1.enter_context(nc.sbuf_tensor("xn", [128, HC * TQ], BF16))
    xsqA = [esA1.enter_context(nc.sbuf_tensor(f"xsqA{i}", [128, TQ], F32)) for i in range(2)]
    qsqb_sb = [esA1.enter_context(nc.sbuf_tensor(f"qsqb{i}", [128, TQ], BF16)) for i in range(2)]
    wqb_sb = [esA1.enter_context(nc.sbuf_tensor(f"wqb{i}", [128, H], BF16)) for i in range(2)]

    def a_consume(cc, buf):
        w5 = min(512, TQ)
        for gq in range(TQ // w5):
            emit_quantT(buf[:, gq * w5:(gq + 1) * w5],
                        xdq_sb[:, cc * TQ + gq * w5:cc * TQ + (gq + 1) * w5], width=w5)

    emit_ln(S - TQ, TQ, xstA, xsqA,
            lambda cc: xn_sb[:, cc * TQ:(cc + 1) * TQ],
            a_consume,
            sumbanks=[(PS_X0, PS_X1), (PS_S0, PS_S1)],
            bcbanks=[(PS_O0, PS_O1), (PS_X0, PS_X1)], srow_sb=srowA_sb, SEG=TQ)
    R.barrier()
    if PHASES == 11:
        return _early_exit([esA1, esA])

    evac_marks = {}
    for h in range(NH):
        w = [(s_v, evac_marks[(h - 2, NQG - 1)])] if h >= 2 else []
        wtok = swc.load(lambda e, h=h: e.dma_start(wqb_sb[h % 2][:, 0:H], wq_cs[h]), waits=w)
        waits = [wtok]
        if h >= 1:
            waits.append((s_v, evac_marks[(h - 1, NQG - 1)]))
        mks = {}
        for k in range(HC):
            for gq in range(NQG):
                bank = SBANK[gq]
                mks[gq] = T(lambda e, h=h, gq=gq, k=k, bank=bank: e.matmul(
                    bank[:, 0:QGW], wqb_sb[h % 2][:, k * 128:(k + 1) * 128],
                    xn_sb[:, k * TQ + gq * QGW:k * TQ + (gq + 1) * QGW],
                    start=(k == 0), stop=(k == HC - 1)),
                  waits=waits if (k == 0 and gq == 0) else (), mark=(k == HC - 1))
        for gq in range(NQG):
            bank = SBANK[gq]
            evac_marks[(h, gq)] = V(lambda e, h=h, gq=gq, bank=bank: e.tensor_scalar_add(
                Q_sb[:, h * TQ + gq * QGW:h * TQ + (gq + 1) * QGW],
                bank[:, 0:QGW], bq_sb[:, h:h + 1]), waits=((s_mm, mks[gq]),))
    R.barrier()

    sqm = {}
    qtm = {}
    for h in range(NH):
        sqm[h] = V(lambda e, h=h: e.tensor_mul(
            qsqb_sb[h % 2][:, 0:TQ], Q_sb[:, h * TQ:(h + 1) * TQ], Q_sb[:, h * TQ:(h + 1) * TQ]),
            waits=[(s_mm, qtm[h - 2])] if h >= 2 else ())
        for gq in range(NQG):
            mk = T(lambda e, h=h, gq=gq: e.matmul(
                [PS_X0, PS_X1][gq][0:NH, 0:QGW], indb_sb[:, h * NH:h * NH + NH],
                qsqb_sb[h % 2][:, gq * QGW:(gq + 1) * QGW],
                start=(h == 0), stop=(h == NH - 1)),
              waits=((s_v, sqm[h]),), mark=True)
        qtm[h] = mk
    V(lambda e: e.reduce_max(stc(C_T0), PS_X0[0:NH, 0:QGW], AX.X), waits=((s_mm, mk),))
    if NQG > 1:
        V(lambda e: e.reduce_max(stc(C_T1), PS_X1[0:NH, 0:QGW], AX.X))
        V(lambda e: e.tensor_max(stc(C_T0), stc(C_T0), stc(C_T1)))
    vm = CN[s_v]
    SC(lambda e: e.activation(stc(C_T0), stc(C_T0), AF.Ln), waits=((s_v, vm),))
    SC(lambda e: e.activation(stc(C_QMAX), stc(C_T0), AF.Exp, scale=0.5))
    R.barrier()
    if PHASES == 12:
        return _early_exit([esA1, esA])
    esA1.close()

    esA2 = ExitStack()
    hh_sb = esA2.enter_context(nc.sbuf_tensor("hh", [128, NHB * TQ], BF16))
    q8w_sb = esA2.enter_context(nc.sbuf_tensor("q8w", [128, 2048], FP8))
    bmw_sb = esA2.enter_context(nc.sbuf_tensor("bmw", [128, 128], F32))
    rsw_sb = esA2.enter_context(nc.sbuf_tensor("rsw", [128, 128], F32))
    s2w_sb = esA2.enter_context(nc.sbuf_tensor("s2w", [128, 128], F32))
    dqr_sb = esA2.enter_context(nc.sbuf_tensor("dqr", [128, 2048], F32))
    wrow_sb = [esA2.enter_context(nc.sbuf_tensor(f"wrow{i}", [128, DQ], F32)) for i in range(2)]
    dqw_sb = [esA2.enter_context(nc.sbuf_tensor(f"dqw{i}", [128, DQ], BF16)) for i in range(2)]
    sg_sb = esA2.enter_context(nc.sbuf_tensor("sg", [128, 2 * QGW], F32))
    hf_sb = esA2.enter_context(nc.sbuf_tensor("hf", [128, QGW], F32))
    dtl_sb = [esA2.enter_context(nc.sbuf_tensor(f"dtl{i}", [128, TQ], F32)) for i in range(2)]

    for half in range(NCH):
        use_marks = {}
        gate_mm = {}
        last_mm = {}
        for j in range(2 * NHB):
            ib = half * NHB + j // 2
            par = j % 2
            w = [(s_v, use_marks[j - 2])] if j >= 2 else []
            rtok = strow.load(lambda e, ib=ib, par=par, j=j: e.dma_start(
                wrow_sb[par][:, 0:H], (wg_rows if j % 2 == 0 else wu_rows)[ib]), waits=w)
            xw = [rtok]
            if j >= 2:
                xw.append((s_mm, last_mm[j - 2]))
            evmj = emit_wquant(H, wrow_sb[par], dqw_sb[par], extra_waits=tuple(xw))
            qm = CN[s_v]
            if FFNLITE < 2:
                last_mm[j] = CN[s_mm]
                use_marks[j] = CN[s_v]
                continue
            sfence = (s_sc, CN[s_sc])
            lastg = -1
            for k in range(HC):
                for gq in range(NQG):
                    bank = [SBANK, [PS_O0, PS_O1]][par][gq]
                    waits = []
                    if k == 0 and gq == 0:
                        waits.append(sfence)
                    if gq == 0 and k // 4 > lastg:
                        lastg = k // 4
                        waits.append((s_v, evmj[lastg]))
                    mk = T(lambda e, par=par, gq=gq, k=k, bank=bank: e.matmul(
                        bank[:, 0:QGW], dqw_sb[par][:, k * 128:(k + 1) * 128],
                        xdq_sb[:, k * TQ + gq * QGW:k * TQ + (gq + 1) * QGW],
                        start=(k == 0), stop=(k == HC - 1)),
                      waits=waits, mark=(k == HC - 1))
                    if k == HC - 1:
                        if j % 2 == 0:
                            gate_mm[(ib, gq)] = mk
                        last_mm[j] = mk
            for gq in range(NQG):
                if j % 2 == 0 or FFNLITE < 3:
                    continue
                if True:
                    gb = SBANK[gq]
                    ub = [PS_O0, PS_O1][gq]
                    scm = SC(lambda e, gb=gb, gq=gq: e.activation(
                        sg_sb[:, gq * QGW:(gq + 1) * QGW], gb[:, 0:QGW], AF.Silu),
                             waits=((s_mm, gate_mm[(ib, gq)]),))
                    V(lambda e, ub=ub, gq=gq: e.tensor_tensor(
                        hf_sb[:, 0:QGW], sg_sb[:, gq * QGW:(gq + 1) * QGW],
                        ub[:, 0:QGW], ALU.mult),
                      waits=((s_sc, scm), (s_mm, mk)))
                    wdt = min(512, QGW)
                    for sg2 in range(QGW // wdt):
                        loc = (ib % NHB) * TQ + gq * QGW + sg2 * wdt
                        emit_quantT(hf_sb[:, sg2 * wdt:(sg2 + 1) * wdt],
                                    hh_sb[:, loc:loc + wdt], width=wdt)
            use_marks[j] = CN[s_v]
        R.barrier()
        if PHASES == 13:
            return _early_exit([esA2, esA])
        duse = {}
        dmms = {}
        evmk = {}
        stmk = {}
        mk_gq = {}
        for m in range(HC):
            for qtr in range(NDQ):
                jj = m * NDQ + qtr
                par = jj % 2
                w = [(s_v, duse[jj - 2])] if jj >= 2 else []
                k0 = half * FFHALF + qtr * DQ
                rtok = strow.load(lambda e, m=m, k0=k0, par=par: e.dma_start(
                    wrow_sb[par][:, 0:DQ], wd_rows[m][:, k0:k0 + DQ]), waits=w)
                xw = [rtok]
                if jj >= 2:
                    xw.append((s_mm, dmms[jj - 2]))
                evmj = emit_wquant(DQ, wrow_sb[par], dqw_sb[par], extra_waits=tuple(xw))
                nk = DQ // 128
                lastg = -1
                for k in range(nk):
                    kg = qtr * nk + k
                    for gq in range(NQG):
                        bank = SBANK[(m * NQG + gq) % 2]
                        waits = []
                        if k == 0 and gq == 0 and qtr == 0:
                            prev = m - (2 if NQG == 1 else 1)
                            if prev >= 0:
                                waits.append((s_v, evmk[prev]))
                        if gq == 0 and k // 4 > lastg:
                            lastg = k // 4
                            waits.append((s_v, evmj[lastg]))
                        mk = T(lambda e, par=par, gq=gq, k=k, kg=kg, bank=bank: e.matmul(
                            bank[:, 0:QGW], dqw_sb[par][:, k * 128:(k + 1) * 128],
                            hh_sb[:, kg * TQ + gq * QGW:kg * TQ + (gq + 1) * QGW],
                            start=(kg == 0), stop=(kg == NHB - 1)),
                          waits=waits, mark=(kg == NHB - 1))
                        mk_gq[gq] = mk
                duse[jj] = CN[s_v]
                dmms[jj] = CN[s_mm]
            if half > 0:
                dtok1 = stm.load(lambda e, m=m: e.dma_start(
                    dtl_sb[m % 2][:, 0:TQ], dacc_hbm[m * 128:(m + 1) * 128, :]),
                   waits=((s_res0, CN[s_res0]), (s_res1, CN[s_res1])) if m <= 1 else
                         [stmk[m - 2]])
            if half == NCH - 1:
                dtok2 = stm.load(lambda e, m=m: e.dma_start(
                    xstA[m % 2][:, 0:TQ], xs[m * 128:(m + 1) * 128, S - TQ:S]))
            for gq in range(NQG):
                bank = SBANK[(m * NQG + gq) % 2]
                sw = [stmk[m - 2]] if m >= 2 else []
                if half == 0:
                    V(lambda e, m=m, gq=gq, bank=bank: e.tensor_copy(
                        dtl_sb[m % 2][:, gq * QGW:(gq + 1) * QGW], bank[:, 0:QGW]),
                      waits=[(s_mm, mk_gq[gq])] + sw)
                else:
                    V(lambda e, m=m, gq=gq, bank=bank: e.tensor_tensor(
                        dtl_sb[m % 2][:, gq * QGW:(gq + 1) * QGW],
                        dtl_sb[m % 2][:, gq * QGW:(gq + 1) * QGW],
                        bank[:, 0:QGW], ALU.add),
                      waits=[(s_mm, mk_gq[gq]), dtok1] + sw)
                    if half == NCH - 1:
                        V(lambda e, m=m, gq=gq: e.tensor_tensor(
                            dtl_sb[m % 2][:, gq * QGW:(gq + 1) * QGW],
                            dtl_sb[m % 2][:, gq * QGW:(gq + 1) * QGW],
                            xstA[m % 2][:, gq * QGW:(gq + 1) * QGW], ALU.add),
                          waits=(dtok2,))
            evmk[m] = vm2 = CN[s_v]
            dst = dacc_hbm if half < NCH - 1 else resid_hbm
            stmk[m] = (s_res[m % 2], R.op("sync", lambda e, m=m, dst=dst: e.dma_start(
                dst[m * 128:(m + 1) * 128, :], dtl_sb[m % 2][:, 0:TQ]),
                waits=((s_v, vm2),), incs=((s_res[m % 2], 16),)))
        R.barrier()
        if PHASES == 14:
            return _early_exit([esA2, esA])
    esA2.close()
    esA.close()

    esB = ExitStack()
    bsb = lambda n, shp, dt: esB.enter_context(nc.sbuf_tensor(n, list(shp), dt))
    O_sb = bsb("O", [128, NH * TQ], F32)
    bvbc_sb = bsb("bvbc", [128, H], BF16)
    K_sb = bsb("K", [128, NH * MEGA], BF16)
    V_sb = bsb("V", [128, KSUB * H], BF16)
    xnm_sb = bsb("xnm", [128, HC * MEGA], BF16)
    wvb_sb = bsb("wvb", [128, (HC // 2) * VFW], BF16)
    wkb_sb = [bsb(f"wkb{i}", [128, H], BF16) for i in range(2)]
    PT_sb = [bsb(f"PT{i}", [128, QGW], BF16) for i in range(4)]
    ksq_sb = bsb("ksq", [128, MEGA], BF16)
    xstB = [bsb(f"xstB{i}", [128, MEGA], F32) for i in range(2)]
    xsqB = [bsb(f"xsqB{i}", [128, MEGA], F32) for i in range(2)]
    srowB_sb = bsb("srowB", [1, 3 * MEGA], F32)

    if PHASES < 2:
        R.op("sync", lambda e: e.nop(),
             waits=((s_res0, CN[s_res0]), (s_res1, CN[s_res1])))
        R.build()
        esB.close()
        es.close()
        nc.finalize()
        nc._final_counts = {getattr(s, 'name', str(s)): v for s, v in CN.items()}
        return nc, c
    resw = ((s_res0, CN[s_res0]), (s_res1, CN[s_res1]))
    R.op("sync", lambda e: e.nop(), waits=resw)
    R.op("gpsimd", lambda e: e.nop(), waits=resw)
    R.op("gpsimd", lambda e: e.dma_start(bvbc_sb[:, :], bv_bc), incs=((s_dmg, 16),))
    bvtok = (s_dmg, CN[s_dmg])
    T(lambda e: e.nop(), waits=resw)
    SC(lambda e: e.nop(), waits=resw)
    V(lambda e: e.memset(O_sb[:, :], 0.0), waits=resw)
    V(lambda e: e.memset(Lacc_sb[:, :], 0.0))
    R.barrier()

    for M in range(NMEGA):
        emit_ln(M * MEGA, MEGA, xstB, xsqB,
                lambda cc: xnm_sb[:, cc * MEGA:(cc + 1) * MEGA],
                None,
                sumbanks=[(PS_X0, PS_X1)],
                bcbanks=[(PS_X0, PS_X1)], srow_sb=srowB_sb, SEG=MEGA)
        R.barrier()
        kevac = {}
        for h in range(NH):
            w = [(s_v, kevac[h - 2])] if h >= 2 else []
            ktok = swc.load(lambda e, h=h: e.dma_start(wkb_sb[h % 2][:, 0:H], wk_cs[h]), waits=w)
            bank = SBANK[h % 2]
            waits = [ktok]
            if h >= 2:
                waits.append((s_v, kevac[h - 2]))
            for k in range(HC):
                mk = T(lambda e, h=h, k=k, bank=bank: e.matmul(
                    bank[:, 0:MEGA], wkb_sb[h % 2][:, k * 128:(k + 1) * 128],
                    xnm_sb[:, k * MEGA:(k + 1) * MEGA],
                    start=(k == 0), stop=(k == HC - 1)),
                  waits=waits if k == 0 else (), mark=(k == HC - 1))
            V(lambda e, h=h, bank=bank: e.tensor_scalar_add(
                K_sb[:, h * MEGA:(h + 1) * MEGA], bank[:, 0:MEGA], bq_sb[:, HC + h:HC + h + 1]),
              waits=((s_mm, mk),))
            kevac[h] = V(lambda e, h=h: e.tensor_mul(
                ksq_sb[:, 0:MEGA], K_sb[:, h * MEGA:(h + 1) * MEGA],
                K_sb[:, h * MEGA:(h + 1) * MEGA]))
            mk = T(lambda e, h=h: e.matmul(
                PS_O0[0:NH, 0:MEGA], indb_sb[:, h * NH:h * NH + NH], ksq_sb[:, 0:MEGA],
                start=(h == 0), stop=(h == NH - 1)),
              waits=((s_v, kevac[h]),), mark=(h == NH - 1))
        if DBG:
            R.op("sync", lambda e, M=M: e.dma_start(
                dbgK[M * 128:(M + 1) * 128, :], K_sb[:, :]),
                waits=((s_v, CN[s_v]),), incs=((s_dm, 16),))
        V(lambda e: e.reduce_max(stc(C_T0), PS_O0[0:NH, 0:MEGA], AX.X), waits=((s_mm, mk),))
        vm = CN[s_v]
        SC(lambda e: e.activation(stc(C_T0), stc(C_T0), AF.Ln), waits=((s_v, vm),))
        scm = SC(lambda e: e.activation(stc(C_KMAX), stc(C_T0), AF.Exp, scale=0.5))
        V(lambda e: e.tensor_mul(stc(C_T1), stc(C_QMAX), stc(C_KMAX)), waits=((s_sc, scm),))
        V(lambda e: e.tensor_scalar_mul(stc(C_T1), stc(C_T1), SCALE))
        if M == 0:
            V(lambda e: e.tensor_copy(stc(C_SB), stc(C_T1)))
        else:
            V(lambda e: e.tensor_max(stc(C_SBN), stc(C_SB), stc(C_T1)))
            vm = V(lambda e: e.tensor_sub(stc(C_T2), stc(C_SB), stc(C_SBN)))
            scm = SC(lambda e: e.activation(stc(C_FAC), stc(C_T2), AF.Exp),
                     waits=((s_v, vm),))
            V(lambda e: e.tensor_copy(stc(C_SB), stc(C_SBN)), waits=((s_sc, scm),))
        if M > 0:
            vm = V(lambda e: e.transpose(rowt_sb[:, :], stat_sb[0:32, C_FAC:C_FAC + 32]))
            tm = T(lambda e: e.matmul(PS_X1[:, 0:NH], ones1_sb[:, :],
                                      rowt_sb[0:1, 0:NH], start=True, stop=True),
                   waits=((s_v, vm),), mark=True)
            V(lambda e: e.tensor_copy(facbc_sb[:, :], PS_X1[:, 0:NH]), waits=((s_mm, tm),))
            for g in range(NQG):
                V(lambda e, g=g: e.tensor_scalar_mul(
                    Lacc_sb[0:NH, g * QGW:(g + 1) * QGW],
                    Lacc_sb[0:NH, g * QGW:(g + 1) * QGW], stc(C_FAC)),
                  waits=((s_dm, CN[s_dm]),))
            for h in range(NH):
                for g in range(NQG):
                    V(lambda e, h=h, g=g: e.tensor_scalar_mul(
                        O_sb[:, h * TQ + g * QGW:h * TQ + (g + 1) * QGW],
                        O_sb[:, h * TQ + g * QGW:h * TQ + (g + 1) * QGW],
                        facbc_sb[:, h:h + 1]))
        V(lambda e: e.transpose(rowt_sb[:, :], stat_sb[0:32, C_SB:C_SB + 32]))
        V(lambda e: e.tensor_scalar(biasrow_sb[0:1, 96:96 + NH], rowt_sb[0:1, 0:NH],
                                    -1.0, CUSH, ALU.mult, ALU.add))
        for ks in range(KSUB):
            V(lambda e, ks=ks, M=M: e.tensor_scalar_add(
                biasrow_sb[0:1, ks * NH:(ks + 1) * NH],
                biasrow_sb[0:1, 96:96 + NH],
                flag_sb[0:1, M * KSUB + ks:M * KSUB + ks + 1]))
        vm = CN[s_v]
        tm = T(lambda e: e.matmul(PS_X0[:, 0:NH * KSUB], ones1_sb[:, :],
                                  biasrow_sb[0:1, 0:NH * KSUB], start=True, stop=True),
               waits=((s_v, vm),), mark=True)
        V(lambda e: e.tensor_copy(bias_sb[:, 0:NH * KSUB], PS_X0[:, 0:NH * KSUB]),
          waits=((s_mm, tm),))
        bias_ready = CN[s_v]
        R.barrier()
        vevac = {}
        vln = {}
        nhalf = HC // 2 if HC > 1 else 1
        for F4 in range(VFG):
            for sub in range(2 if HC > 1 else 1):
                idx = F4 * 2 + sub if HC > 1 else F4
                w = [(s_mm, vln[idx - 1])] if idx >= 1 else []
                if idx >= 1:
                    w.append((s_v, vevac[idx - 1]))
                vtok = swv.load(lambda e, F4=F4, c0=sub * nhalf: e.dma_start(
                    wvb_sb[:, 0:nhalf * VFW],
                    wv_rs[F4][:, c0 * VFW:(c0 + nhalf) * VFW]), waits=w)
                c0 = sub * nhalf
                for ks in range(KSUB):
                    bank = [PS_O0, PS_O1, PS_X0, PS_X1][ks]
                    for ci in range(nhalf):
                        k = c0 + ci
                        mk = T(lambda e, k=k, ci=ci, ks=ks, bank=bank: e.matmul(
                            bank[:, 0:VFW], xnm_sb[:, k * MEGA + ks * 128:k * MEGA + ks * 128 + 128],
                            wvb_sb[:, ci * VFW:(ci + 1) * VFW],
                            start=(k == 0), stop=(k == HC - 1)),
                          waits=(vtok,) if ci == 0 else (),
                          mark=(ci == nhalf - 1))
                    if sub == (1 if HC > 1 else 0):
                        V(lambda e, F4=F4, ks=ks, bank=bank: e.scalar_tensor_tensor(
                            V_sb[:, ks * H + F4 * VFW:ks * H + (F4 + 1) * VFW],
                            bank[:, 0:VFW], 1.0,
                            bvbc_sb[:, F4 * VFW:(F4 + 1) * VFW], ALU.mult, ALU.add),
                          waits=((s_mm, mk), bvtok))
                vevac[idx] = CN[s_v]
                vln[idx] = CN[s_mm]
        vready = CN[s_v]
        R.barrier()
        exp_marks = {}
        pv_marks = {}
        oevac = {}
        lmk = {}
        for h in range(NH):
            for ks in range(KSUB):
                kslot = M * KSUB + ks
                kpar = ks % 2
                j = kslot - OWN0
                mkq = {}
                for g in range(NQG):
                    bidx = kpar * 2 + g
                    sbank = [PS_S0, PS_S1, PS_X0, PS_X1][bidx]
                    waits = [(s_v, vready)]
                    if bidx in exp_marks:
                        waits.append((s_sc, exp_marks[bidx]))
                    mkq[g] = T(lambda e, h=h, g=g, ks=ks, sbank=sbank: e.matmul(
                        sbank[:, 0:QGW], K_sb[:, h * MEGA + ks * 128:h * MEGA + ks * 128 + 128],
                        Q_sb[:, h * TQ + g * QGW:h * TQ + (g + 1) * QGW],
                        start=True, stop=True),
                      waits=waits, mark=True)
                for g in range(NQG):
                    bidx = kpar * 2 + g
                    sbank = [PS_S0, PS_S1, PS_X0, PS_X1][bidx]
                    trim = None
                    if j >= 0:
                        for qq in range(QSUBG):
                            if j == g * QSUBG + qq:
                                trim = V(lambda e, qq=qq, sbank=sbank: e.tensor_tensor(
                                    sbank[:, qq * 128:(qq + 1) * 128],
                                    sbank[:, qq * 128:(qq + 1) * 128],
                                    tri_sb[:, :], ALU.add), waits=((s_mm, mkq[g]),))
                    ew = [(s_mm, mkq[g]), (s_v, max(bias_ready, trim or 0))]
                    if (kpar, g) in pv_marks:
                        ew.append((s_mm, pv_marks[(kpar, g)]))
                    em = SC(lambda e, ks=ks, h=h, sbank=sbank, bidx=bidx: e.activation(
                        PT_sb[bidx][:, 0:QGW], sbank[:, 0:QGW], AF.Exp,
                        bias=bias_sb[:, ks * NH + h:ks * NH + h + 1], scale=SCALE),
                        waits=ew)
                    exp_marks[bidx] = em
                    memm = None
                    if j >= 0:
                        for qq in range(QSUBG):
                            if j > g * QSUBG + qq:
                                memm = V(lambda e, qq=qq, bidx=bidx: e.memset(
                                    PT_sb[bidx][:, qq * 128:(qq + 1) * 128], 0.0),
                                    waits=((s_sc, em),))
                    exp_marks[(bidx, "mem")] = memm
                for g in range(NQG):
                    bidx = kpar * 2 + g
                    obank = [PS_O0, PS_O1][g]
                    pvw = [(s_sc, exp_marks[bidx])]
                    if exp_marks.get((bidx, "mem")):
                        pvw.append((s_v, exp_marks[(bidx, "mem")]))
                    if ks == 0 and (h - 1, g) in oevac:
                        pvw.append((s_v, oevac[(h - 1, g)]))
                    T(lambda e, ks=ks, h=h, bidx=bidx, obank=obank: e.matmul(
                        obank[:, 0:QGW], V_sb[:, ks * H + h * 128:ks * H + h * 128 + 128],
                        PT_sb[bidx][:, 0:QGW], start=(ks == 0), stop=(ks == KSUB - 1)),
                      waits=pvw)
                for g in range(NQG):
                    bidx = kpar * 2 + g
                    mk2 = T(lambda e, ks=ks, h=h, g=g, bidx=bidx: e.matmul(
                        LBANK[g][0:NH, 0:QGW], indb_sb[:, h * NH:h * NH + NH],
                        PT_sb[bidx][:, 0:QGW],
                        start=(h == 0 and ks == 0),
                        stop=(h == NH - 1 and ks == KSUB - 1)),
                      mark=True)
                    pv_marks[(kpar, g)] = mk2
                    lmk[g] = mk2
            for g in range(NQG):
                obank = [PS_O0, PS_O1][g]
                oevac[(h, g)] = V(lambda e, h=h, g=g, obank=obank: e.tensor_tensor(
                    O_sb[:, h * TQ + g * QGW:h * TQ + (g + 1) * QGW],
                    O_sb[:, h * TQ + g * QGW:h * TQ + (g + 1) * QGW],
                    obank[:, 0:QGW], ALU.add), waits=((s_mm, pv_marks[(KSUB - 1) % 2, g] if False else lmk[g]),))
        for g in range(NQG):
            V(lambda e, g=g: e.tensor_tensor(
                Lacc_sb[0:NH, g * QGW:(g + 1) * QGW],
                Lacc_sb[0:NH, g * QGW:(g + 1) * QGW],
                LBANK[g][0:NH, 0:QGW], ALU.add), waits=((s_mm, lmk[g]),))
        if DBG:
            R.op("sync", lambda e, M=M: e.dma_start(
                dbgL[M * 16:M * 16 + 16, :], Lacc_sb[:, :]),
                waits=((s_v, CN[s_v]),), incs=((s_dm, 16),))
        R.barrier()
    esB.close()
    if PHASES < 3:
        R.op("sync", lambda e: e.nop(),
             waits=((s_res0, CN[s_res0]), (s_res1, CN[s_res1]), (s_dm, CN[s_dm]), (s_dmg, CN[s_dmg])))
        R.build()
        es.close()
        nc.finalize()
        nc._final_counts = {getattr(s, 'name', str(s)): v for s, v in CN.items()}
        return nc, c

    esC = ExitStack()
    csb = lambda n, shp, dt: esC.enter_context(nc.sbuf_tensor(n, list(shp), dt))
    Obf_sb = csb("Obf", [128, NH * TQ], BF16)
    wob_sb = [csb(f"wob{i}", [128, H], BF16) for i in range(2)]
    ytl_sb = [csb(f"ytl{i}", [128, QGW], F32) for i in range(2)]
    rtl_sb = [csb(f"rtl{i}", [128, TQ], F32) for i in range(2)]

    for g in range(NQG):
        V(lambda e, g=g: e.reciprocal(rL_sb[0:NH, g * QGW:(g + 1) * QGW],
                                      Lacc_sb[0:NH, g * QGW:(g + 1) * QGW]))
    vm = CN[s_v]
    srl2 = Dstream("srl2", "gpsimd")
    bcmk = {}
    for h in range(NH):
        for g in range(NQG):
            i = h * NQG + g
            slot = i % 2
            bank = [PS_X0, PS_X1][i % 2]
            w = [(s_v, vm)]
            if i >= 2:
                w.append((s_mm, bcmk[i - 2]))
            ltok = srl2.load(lambda e, h=h, g=g, slot=slot: e.dma_start(
                rl0_sb[0:1, slot * QGW:(slot + 1) * QGW],
                rL_sb[h:h + 1, g * QGW:(g + 1) * QGW]), waits=w)
            w2 = [ltok]
            if i >= 2:
                w2.append((s_v, CN[s_v]))
            tm = T(lambda e, slot=slot, bank=bank: e.matmul(
                bank[:, 0:QGW], ones1_sb[:, :],
                rl0_sb[0:1, slot * QGW:(slot + 1) * QGW],
                start=True, stop=True), waits=w2, mark=True)
            bcmk[i] = tm
            V(lambda e, h=h, g=g, bank=bank: e.tensor_tensor(
                Obf_sb[:, h * TQ + g * QGW:h * TQ + (g + 1) * QGW],
                O_sb[:, h * TQ + g * QGW:h * TQ + (g + 1) * QGW],
                bank[:, 0:QGW], ALU.mult), waits=((s_mm, tm),))
    R.barrier()
    yev = {}
    otmk = {}
    for m in range(HC):
        w = [(s_v, yev[(m - 2, NQG - 1)])] if m >= 2 else []
        wtok = swc.load(lambda e, m=m: e.dma_start(wob_sb[m % 2][:, 0:H], wo_cs[m]), waits=w)
        ow = [otmk[m - 2]] if m >= 2 else []
        rtok = stm.load(lambda e, m=m: e.dma_start(rtl_sb[m % 2][:, 0:TQ],
                                                   resid_hbm[m * 128:(m + 1) * 128, :]),
                        waits=[(s_res0, CN[s_res0]), (s_res1, CN[s_res1])] + w + ow)
        waits = [wtok]
        if m >= 1:
            waits.append((s_v, yev[(m - 1, NQG - 1)]))
        mks = {}
        for k in range(HC):
            for gq in range(NQG):
                bank = SBANK[gq]
                mks[gq] = T(lambda e, m=m, gq=gq, k=k, bank=bank: e.matmul(
                    bank[:, 0:QGW], wob_sb[m % 2][:, k * 128:(k + 1) * 128],
                    Obf_sb[:, k * TQ + gq * QGW:k * TQ + (gq + 1) * QGW],
                    start=(k == 0), stop=(k == HC - 1)),
                  waits=waits if (k == 0 and gq == 0) else (), mark=(k == HC - 1))
        for gq in range(NQG):
            bank = SBANK[gq]
            V(lambda e, m=m, gq=gq, bank=bank: e.tensor_scalar_add(
                ytl_sb[gq % 2][:, 0:QGW], bank[:, 0:QGW], bo_sb[:, m:m + 1]),
              waits=((s_mm, mks[gq]), rtok))
            yev[(m, gq)] = V(lambda e, m=m, gq=gq: e.tensor_tensor(
                rtl_sb[m % 2][:, gq * QGW:(gq + 1) * QGW],
                rtl_sb[m % 2][:, gq * QGW:(gq + 1) * QGW],
                ytl_sb[gq % 2][:, 0:QGW], ALU.add))
        vm2 = CN[s_v]
        otmk[m] = (s_out[m % 2], R.op("sync", lambda e, m=m: e.dma_start(
            outT[m * 128:(m + 1) * 128, :], rtl_sb[m % 2][:, 0:TQ]),
            waits=((s_v, vm2),), incs=((s_out[m % 2], 16),)))
    R.op("sync", lambda e: e.nop(),
         waits=((s_out0, CN[s_out0]), (s_out1, CN[s_out1]), (s_dm, CN[s_dm])))

    R.build()
    esC.close()
    es.close()
    nc.finalize()
    nc._final_counts = {getattr(s, 'name', str(s)): v for s, v in CN.items()}
    return nc, c


def stage_inputs(cfg, x, Wqkv, bqkv, Wout, bout, gw, uw, dw):
    c = derive(cfg)
    H, NH, TQ, S, FFI = c["H"], c["NH"], c["TQ"], c["S"], c["FFI"]
    HC, VFW, VFG, FFB = c["HC"], c["VFW"], c["VFG"], c["FFB"]
    NKSLOT, OWN0 = c["NKSLOT"], c["OWN0"]
    f32 = np.float32
    ac = np.ascontiguousarray

    def colslab(W, nb):
        K = W.shape[1]
        return ac(W.reshape(nb, 128, K // 128, 128).transpose(0, 3, 2, 1)
                  .reshape(nb, 128, K)).astype(f32)

    Wq, Wk, Wv = Wqkv[:H], Wqkv[H:2 * H], Wqkv[2 * H:]
    shared = {
        "wq_cs": colslab(Wq, NH),
        "wk_cs": colslab(Wk, NH),
        "wv_rs": ac(Wv.reshape(VFG, VFW, HC, 128).transpose(0, 3, 2, 1)
                    .reshape(VFG, 128, HC * VFW)).astype(f32),
        "wo_cs": colslab(Wout, HC),
        "wg_rows": ac(gw.reshape(FFB, 128, H)).astype(f32),
        "wu_rows": ac(uw.reshape(FFB, 128, H)).astype(f32),
        "wd_rows": ac(dw.reshape(HC, 128, FFI)).astype(f32),
        "bqkv_t": ac(bqkv.reshape(3 * HC, 128).T).astype(f32),
        "bout_t": ac(bout.reshape(HC, 128).T).astype(f32),
        "bv_bc": ac(np.broadcast_to(bqkv[2 * H:3 * H], (128, H))).astype(np.float32).astype(ml_dtypes.bfloat16),
        "tri_in": np.where(np.arange(128)[:, None] <= np.arange(128)[None, :],
                        np.float32(0.0), np.float32(NEG)),
    }
    ind = np.zeros((128, NH * NH), dtype=f32)
    for h in range(NH):
        ind[:, h * NH + h] = 1.0
    shared["ind_b"] = ind.astype(ml_dtypes.bfloat16)
    shared["id_in"] = np.eye(128, dtype=f32)

    NCHUNK = S // TQ
    in_maps = []
    for core in range(2 * NCHUNK):
        b, j = core // NCHUNK, core % NCHUNK
        xb = np.asarray(x[b], dtype=f32)
        plen = j * TQ
        xsT = np.zeros((H, S), dtype=f32)
        if plen:
            xsT[:, :plen] = xb[:plen].T
        xsT[:, S - TQ:] = xb[j * TQ:(j + 1) * TQ].T
        flag = np.zeros((128, NKSLOT), dtype=f32)
        flag[:, plen // 128:OWN0] = NEG
        m = dict(shared)
        m["xs"] = ac(xsT)
        m["flag_in"] = flag
        in_maps.append(m)
    return in_maps


_NC_CACHE = {}


def kernel(L0_attn_x, L0_attn_W_qkv, L0_attn_b_qkv, L0_attn_W_out, L0_attn_b_out,
           L0_ffn_gate_w, L0_ffn_up_w, L0_ffn_down_w):
    cfg = FULL_CFG
    c = derive(cfg)
    x = np.asarray(L0_attn_x, dtype=np.float32)
    in_maps = stage_inputs(cfg, x,
                           np.asarray(L0_attn_W_qkv, dtype=np.float32),
                           np.asarray(L0_attn_b_qkv, dtype=np.float32),
                           np.asarray(L0_attn_W_out, dtype=np.float32),
                           np.asarray(L0_attn_b_out, dtype=np.float32),
                           np.asarray(L0_ffn_gate_w, dtype=np.float32),
                           np.asarray(L0_ffn_up_w, dtype=np.float32),
                           np.asarray(L0_ffn_down_w, dtype=np.float32))
    key = tuple(sorted(cfg.items()))
    if key not in _NC_CACHE:
        _NC_CACHE[key] = build_nc(cfg)
    nc, _ = _NC_CACHE[key]
    res = run_bass_kernel_spmd(nc, in_maps, core_ids=list(range(8)))
    TQ = c["TQ"]
    out = np.empty((2, c["S"], c["H"]), dtype=np.float32)
    for core in range(8):
        b, j = core // 4, core % 4
        out[b, j * TQ:(j + 1) * TQ, :] = np.asarray(res.results[core]["outT"]).T
    return out
